# version 08
# baseline (speedup 1.0000x reference)
import contextlib
import numpy as np
import ml_dtypes
import concourse.bass as bass
import concourse.mybir as mybir
from concourse.bass_utils import run_bass_kernel_spmd

F32 = mybir.dt.float32
BF16 = mybir.dt.bfloat16
I32 = mybir.dt.int32
MAGIC_F = float(np.int32(0x5f3759df).view(np.float32))
AF = mybir.ActivationFunctionType
ALU = mybir.AluOpType
AX = mybir.AxisListType

D = 2048
H = 16
DK = 64
DV = 128
M = 64
KD = H * DK
VD = H * DV
MD = H * M
FF = 5120
C = 128
EPS = 1e-5
CLAMP = 32.0
SCALE = DK ** -0.5
NCORE = 8
DCH = D // 128
NFT = FF // 128


class Cfg:
    def __init__(self, T):
        assert T % (NCORE * 512) == 0
        self.T = T
        self.TC = T // NCORE
        self.TP = self.TC * (NCORE - 1)
        self.NPRE = self.TP // C
        self.NOWN = self.TC // C
        self.NGRP = self.TC // 512
        self.PREG = 4
        assert self.NPRE % self.PREG == 0


class _SemRef:
    __slots__ = ("cell",)

    def __init__(self, cell):
        self.cell = cell

    @property
    def h(self):
        return self.cell[0]

    @property
    def n(self):
        return self.cell[1]

    def inc(self, instr, k=1):
        instr.then_inc(self.cell[0], k)
        self.cell[1] += k
        return instr

    def wait(self, eng, lag=0):
        if self.cell[1] - lag > 0:
            eng.wait_ge(self.cell[0], self.cell[1] - lag)

    def wait_at(self, eng, val):
        if val > 0:
            eng.wait_ge(self.cell[0], val)


class Sems:
    def __init__(self, nc, stack):
        object.__setattr__(self, "_nc", nc)
        object.__setattr__(self, "_stack", stack)
        object.__setattr__(self, "_tab", {})

    def __getattr__(self, name):
        tab = object.__getattribute__(self, "_tab")
        if name not in tab:
            nc = object.__getattribute__(self, "_nc")
            stack = object.__getattribute__(self, "_stack")
            h = stack.enter_context(nc.semaphore(name))
            tab[name] = [h, 0]
        return _SemRef(tab[name])


class SlotPool:

    def __init__(self, n, sem):
        self.n = n
        self.sem = sem
        self.marks = [0] * n
        self.idx = 0

    def acquire(self, eng):
        i = self.idx % self.n
        self.sem.wait_at(eng, self.marks[i])
        self.idx += 1
        return i

    def release(self, i):
        self.marks[i % self.n] = self.sem.n


class Rot:

    def __init__(self, n, sem):
        self.n = n
        self.sem = sem
        self.idx = 0
        self.base = sem.n

    def acquire(self, eng):
        self.sem.wait_at(eng, self.base + self.idx - self.n + 1)

    def emitted(self):
        i = self.idx % self.n
        self.idx += 1
        return i


def build_program(cfg, debug=False):
    nc = bass.Bass("TRN2", target_bir_lowering=False, debug=False)
    TC, TP, NPRE, NOWN, NGRP, PREG = (
        cfg.TC, cfg.TP, cfg.NPRE, cfg.NOWN, cfg.NGRP, cfg.PREG)
    NT4 = TC // 512

    ap = lambda t: t.ap()
    I = lambda n, s, d=F32: ap(nc.dram_tensor(n, s, d, kind="ExternalInput"))
    xpre_tok = I("xpre_tok", [TP, D])
    xpreT = I("xpreT", [D, TP])
    xown_tok = I("xown_tok", [TC, D])
    xownT = I("xownT", [D, TC])
    vbias = I("vbias", [128, NPRE])
    wq_h = I("wq", [D, KD])
    wk_h = I("wk", [D, KD])
    wv_h = I("wv", [D, VD])
    wg_h = I("wg", [D, VD])
    ws_h = I("ws", [D, MD])
    wo_h = I("wo", [VD, D])
    wfcT_h = I("wfcT", [D, FF])
    wprojT_h = I("wprojT", [FF, D])
    gnw_h = I("gnw", [1, DV])

    outT_h = ap(nc.dram_tensor("outT", [D, TC], F32, kind="ExternalOutput"))

    dk = "ExternalOutput" if debug else "Internal"
    d_q = ap(nc.dram_tensor("d_q", [NOWN, 128, KD], BF16, kind=dk))
    d_kT = ap(nc.dram_tensor("d_kT", [NOWN, 128, KD], BF16, kind=dk))
    d_kt = ap(nc.dram_tensor("d_kt", [NOWN, 128, KD], BF16, kind=dk))
    d_ef = ap(nc.dram_tensor("d_ef", [NOWN, 128, MD], BF16, kind=dk))
    d_et = ap(nc.dram_tensor("d_et", [NOWN, 128, MD], BF16, kind=dk))
    d_v = ap(nc.dram_tensor("d_v", [NOWN, 128, VD], BF16, kind=dk))
    d_sg = ap(nc.dram_tensor("d_sg", [NOWN, 128, VD], BF16, kind=dk))
    d_hr = ap(nc.dram_tensor("d_hr", [NFT, 128, TC], BF16, kind=dk))
    d_wpt = ap(nc.dram_tensor("d_wpt", [DCH, 128, NFT * 128], BF16, kind=dk))
    d_wo = ap(nc.dram_tensor("d_wo", [DCH, 128, DCH * 128], BF16, kind=dk))
    if debug:
        d_X = ap(nc.dram_tensor("d_X", [128, DCH * MD], F32, kind=dk))
        d_A = ap(nc.dram_tensor("d_A", [1, MD], F32, kind=dk))
        d_SK = ap(nc.dram_tensor("d_SK", [128, 8 * M], F32, kind=dk))
        d_SV = ap(nc.dram_tensor("d_SV", [128, 8 * DV], F32, kind=dk))
        d_xnT = ap(nc.dram_tensor("d_xnT", [128, DCH * TC], BF16, kind=dk))
        d_xat1 = ap(nc.dram_tensor("d_xat1", [NGRP, 128, DCH * 512], F32, kind=dk))
        d_ogT = ap(nc.dram_tensor("d_ogT", [NGRP, 128, DCH * 512], BF16, kind=dk))

    st = contextlib.ExitStack()
    with st:
        sb = lambda n, s, d: st.enter_context(nc.sbuf_tensor(n, s, d))
        S = Sems(nc, st)
        block = st.enter_context(nc.Block())

        mask01 = sb("mask01", [128, 128], BF16)
        ident = sb("ident", [128, 128], BF16)
        iota_f = sb("iota_f", [128, 128], F32)
        ones_bf = sb("ones_bf", [128, 128], BF16)
        onesc_f32 = sb("onesc_f32", [128, 1], F32)
        ones1_f32 = sb("ones1_f32", [1, 128], F32)
        eps_col = sb("eps_col", [128, 1], F32)
        zero_col = sb("zero_col", [128, 1], F32)
        gnw_bc = sb("gnw_bc", [128, DV], F32)
        vb_sb = sb("vb_sb", [128, NPRE], F32)
        thr_col = sb("thr_col", [128, 2], F32)
        nthr_col = sb("nthr_col", [128, 2], F32)
        amc_col = sb("amc_col", [128, 1], F32)
        srow = sb("srow", [1, 16], F32)
        SKs = sb("SKs", [128, 8 * M], F32)
        SVs = sb("SVs", [128, 8 * DV], F32)
        A_row = sb("A_row", [1, MD], F32)
        bnst = sb("bnst", [128, 24], F32)
        mv = sb("mv", [128, 2], F32)
        rstd_c = sb("rstd_c", [128, 1], F32)
        nmean_c = sb("nmean_c", [128, 1], F32)
        nrmu_c = sb("nrmu_c", [128, 1], F32)
        mgc = sb("mgc", [128, 1], F32)
        qk_a = sb("qk_a", [128, 128], F32)
        qk_b = sb("qk_b", [128, 128], F32)

        ACT_SET_ON = False

        def act_set(s, idx=6):
            if not ACT_SET_ON:
                return
            s.add_instruction(mybir.InstLoadActFuncSet(
                name=nc.get_next_instruction_name(),
                act_func_set_id=idx, ins=[], outs=[]))

        def barrier():
            tgt = S.bar.n + 5
            for sec in (block.sync, block.gpsimd, block.scalar, block.vector,
                        block.tensor):
                def _f(e, tgt=tgt):
                    S.bar.inc(e.drain(), 1)
                    e.wait_ge(S.bar.h, tgt)
                sec(_f)

        def rsqrt_dve(v, out, q1, q2, n):
            v.tensor_scalar(q2.bitcast(I32), q1.bitcast(I32), 1, None,
                            ALU.arith_shift_right)
            v.tensor_tensor(q2.bitcast(I32),
                            mgc.ap().bitcast(I32).broadcast_to([128, n]),
                            q2.bitcast(I32), ALU.subtract)
            for it in range(3):
                v.tensor_tensor(out, q1, q2, ALU.mult)
                v.tensor_tensor(out, out, q2, ALU.mult)
                v.tensor_scalar(out, out, -0.5, 1.5, ALU.mult, ALU.add)
                if it < 2:
                    v.tensor_tensor(q2, q2, out, ALU.mult)
                else:
                    S.ve.inc(v.tensor_tensor(out, q2, out, ALU.mult), 1)

        def bcast_row(row_ap, out_sb_ap, ps_ap, extra=()):
            @block.tensor
            def _(t):
                S.ve.wait(t)
                for s_ in extra:
                    s_.wait(t)
                S.te.inc(t.matmul(ps_ap, ones1_f32[:], row_ap,
                                  start=True, stop=True), 1)

            @block.vector
            def _(v):
                S.te.wait(v)
                S.ve.inc(v.tensor_copy(out_sb_ap, ps_ap), 1)

        @block.gpsimd
        def _(g):
            g.iota(iota_f[:], [[1, 128]], base=0, channel_multiplier=-1,
                   allow_small_or_imprecise_dtypes=True)
            g.memset(ones_bf[:], 1.0)
            g.memset(ones1_f32[:], 1.0)
            g.memset(onesc_f32[:], 1.0)
            g.memset(eps_col[:], EPS)
            g.memset(zero_col[:], 0.0)
            g.memset(mgc[:], MAGIC_F)
            S.gp.inc(g.affine_select(ident[:], ones_bf[:], [[-1, 128]],
                                     ALU.is_equal, 0.0, base=0,
                                     channel_multiplier=1), 1)

        @block.sync
        def _(sy):
            S.ld.inc(sy.dma_start(out=vb_sb[:], in_=vbias[:]), 16)

        @block.scalar
        def _(s):
            act_set(s, 6)

        @block.vector
        def _(v):
            S.gp.wait(v)
            S.ve.inc(v.tensor_scalar(mask01[:], iota_f[:], 0.0, None, ALU.is_ge), 1)
            v.memset(SKs[:], 0.0)
            v.memset(SVs[:], 0.0)
            S.ve.inc(v.memset(A_row[:], 0.0), 1)

        with nc.sbuf_tensor("gnw_sb", [1, DV], F32) as gnw_sb, \
                nc.psum_tensor("ps_g0", [128, 512], F32) as ps_g0:
            @block.sync
            def _(sy):
                S.ld.inc(sy.dma_start(out=gnw_sb[:], in_=gnw_h[:]), 16)

            @block.tensor
            def _(t):
                S.ld.wait(t)
                S.gp.wait(t)
                S.te.inc(t.matmul(ps_g0[:, 0:DV], ones1_f32[:], gnw_sb[:],
                                  start=True, stop=True), 1)

            @block.vector
            def _(v):
                S.te.wait(v)
                S.ve.inc(v.tensor_scalar(gnw_bc[:], ps_g0[:, 0:DV], 0.5,
                                         None, ALU.mult), 1)

        barrier()

        p12 = st.enter_context(contextlib.ExitStack())
        X_bf = p12.enter_context(nc.sbuf_tensor("X_bf", [128, DCH * MD], BF16))
        rstd_w = p12.enter_context(nc.sbuf_tensor("rstd_w", [128, 128], F32))
        rmu_w = p12.enter_context(nc.sbuf_tensor("rmu_w", [128, 128], F32))

        with contextlib.ExitStack() as p1:
            sb1 = lambda n, s, d: p1.enter_context(nc.sbuf_tensor(n, s, d))
            ps1 = lambda n, s, d: p1.enter_context(nc.psum_tensor(n, s, d))
            ws_bf = sb1("ws_bf", [128, DCH * MD], BF16)
            csw_row = sb1("csw_row", [1, MD], F32)
            csw_bc = sb1("csw_bc", [128, MD], F32)
            X_sb = sb1("X_sb", [128, DCH * MD], F32)
            crow_sb = csw_row
            cbc_sb = csw_bc
            xtok2 = [sb1(f"xtok{i}", [128, D], F32) for i in range(2)]
            xTb2 = [sb1(f"xTb{i}", [128, D], BF16) for i in range(2)]
            xp_g = sb1("xp_g", [128, PREG * D], BF16)
            e_g = sb1("e_g", [128, PREG * MD], BF16)
            onemu_g = sb1("onemu_g", [128, 2 * PREG], BF16)
            s_st1 = sb1("s_st1", [128, MD], F32)
            s_bfc = [sb1(f"s_bfc{i}", [128, MD], BF16)
                     for i in range(2)]

            ps_s = ps1("ps_s", [128, MD], F32)
            ps_x = ps1("ps_x", [128, MD], F32)
            ps_x2 = ps1("ps_x2", [128, MD], F32)
            ps_a1 = ps1("ps_a1", [128, 512], F32)
            ps_a2 = ps1("ps_a2", [128, 512], F32)

            @block.gpsimd
            def _(g):
                S.ca.inc(g.dma_start(
                    out=ws_bf.ap().rearrange("p (c f) -> p c f", f=MD),
                    in_=ws_h.rearrange("(c p) f -> p c f", p=128)), 16)
                for ci in range(PREG):
                    g.memset(onemu_g[:, 2 * ci:2 * ci + 1], 1.0)
                S.gp.inc(g.engine_nop(), 1)

            for half in range(2):
                @block.tensor
                def _(t, half=half):
                    S.ca.wait(t)
                    if half == 1:
                        S.ve.wait(t)
                    for dc in range(DCH):
                        ii = t.matmul(
                            ps_a1[0:1, 0:512], ones_bf[:, 0:1],
                            ws_bf[:, dc * MD + half * 512:dc * MD + half * 512 + 512],
                            start=(dc == 0), stop=(dc == DCH - 1))
                    S.te.inc(ii, 1)

                @block.vector
                def _(v, half=half):
                    S.te.wait(v)
                    S.ve.inc(v.tensor_copy(csw_row[0:1, half * 512:half * 512 + 512],
                                           ps_a1[0:1, 0:512]), 1)

            bcast_row(csw_row[0:1, 0:512], csw_bc[:, 0:512], ps_s[:, 0:512])
            bcast_row(csw_row[0:1, 512:MD], csw_bc[:, 512:MD], ps_s[:, 0:512])

            @block.vector
            def _(v):
                v.memset(X_sb[:], 0.0)
                S.ve.inc(v.engine_nop(), 1)

            barrier()

            ngrp_pre = NPRE // PREG
            p1_exp_jobs = []
            onemu_marks = {}
            sp_marks = {}

            def emit_amm(ch):
                @block.tensor
                def _(t, ch=ch):
                    ci = ch % PREG
                    S.sev.wait_at(t, ch + 1)
                    S.ve.wait_at(t, onemu_marks[ch])
                    t.matmul(ps_a1[0:2, 0:512],
                             onemu_g[:, 2 * ci:2 * ci + 2],
                             e_g[:, ci * MD:ci * MD + 512],
                             start=(ch == 0), stop=(ch == NPRE - 1),
                             skip_group_check=True)
                    ii = t.matmul(ps_a2[0:2, 0:512],
                                  onemu_g[:, 2 * ci:2 * ci + 2],
                                  e_g[:, ci * MD + 512:(ci + 1) * MD],
                                  start=(ch == 0), stop=(ch == NPRE - 1),
                                  skip_group_check=True)
                    S.te.inc(ii, 1)

            def emit_xaccum(gi):
                @block.tensor
                def _(t, gi=gi):
                    S.sev.wait_at(t, (gi + 1) * PREG)
                    S.ve.wait(t)
                    for dc in range(DCH):
                        pp = ps_x if dc % 2 == 0 else ps_x2
                        S.xev.wait_at(t, gi * DCH + dc - 1)
                        for c2 in range(PREG):
                            for half in range(2):
                                ii = t.matmul(
                                    pp[:, half * 512:(half + 1) * 512],
                                    xp_g[:, c2 * D + dc * C:c2 * D + (dc + 1) * C],
                                    e_g[:, c2 * MD + half * 512:
                                        c2 * MD + (half + 1) * 512],
                                    start=(c2 == 0), stop=(c2 == PREG - 1),
                                    skip_group_check=True)
                        S.te.inc(ii, 1)

                @block.vector
                def _(v, gi=gi):
                    base = S.te.n - DCH
                    for dc in range(DCH):
                        pp = ps_x if dc % 2 == 0 else ps_x2
                        S.te.wait_at(v, base + dc + 1)
                        ii = v.tensor_tensor(X_sb[:, dc * MD:(dc + 1) * MD],
                                             X_sb[:, dc * MD:(dc + 1) * MD],
                                             pp[:], ALU.add)
                        S.xev.inc(ii, 1)

            for ch in range(NPRE):
                gi, ci = divmod(ch, PREG)
                buf = ch % 2

                @block.sync
                def _(sy, ch=ch, buf=buf):
                    S.pxf.wait_at(sy, ch - 1)
                    S.ld.inc(sy.dma_start(
                        out=xtok2[buf][:],
                        in_=xpre_tok[ch * C:(ch + 1) * C, :]), 16)

                @block.gpsimd
                def _(g, ch=ch, buf=buf):
                    S.sev.wait_at(g, ch - 1)
                    S.ca.inc(g.dma_start(
                        out=xTb2[buf].ap().rearrange("p (c t) -> p c t", t=C),
                        in_=xpreT[:, ch * C:(ch + 1) * C]
                        .rearrange("(c p) t -> p c t", p=128)), 16)

                @block.tensor
                def _(t, ch=ch, buf=buf, ci=ci, gi=gi):
                    S.ca.wait(t)
                    S.sev.wait_at(t, ch - 1)
                    if ci == 1 and gi > 0:
                        S.xev.wait_at(t, gi * DCH)
                    pp = ps_s if ch % 2 == 0 else ps_x2
                    for half in range(2):
                        for dc in range(DCH):
                            ii = t.matmul(
                                pp[:, half * 512:(half + 1) * 512],
                                xTb2[buf][:, dc * C:(dc + 1) * C],
                                ws_bf[:, dc * MD + half * 512:
                                      dc * MD + (half + 1) * 512],
                                start=(dc == 0), stop=(dc == DCH - 1))
                    S.te.inc(ii, 1)

                sp_marks[ch] = S.te.n
                if ch > 0:
                    emit_amm(ch - 1)
                if ci == 0 and gi > 0:
                    emit_xaccum(gi - 1)

                @block.vector
                def _(v, ch=ch, buf=buf):
                    S.ld.wait(v)
                    for j in range(4):
                        ii = v.bn_stats(bnst[:, j * 6:(j + 1) * 6],
                                        xtok2[buf][:, j * 512:(j + 1) * 512])
                    S.ve.inc(ii, 1)
                    S.ve.wait(v)
                    S.ve.inc(v.bn_aggr(mv[:], bnst[:]), 1)

                @block.vector
                def _(v, ch=ch, buf=buf, ci=ci):
                    v.tensor_copy(qk_b[:], gnw_bc[:, 0:128])
                    v.tensor_scalar(qk_a[:],
                                    mv[:, 1:2].broadcast_to([128, 128]),
                                    EPS, None, ALU.add)
                    rsqrt_dve(v, rstd_w[:], qk_a[:], qk_b[:], 128)
                    v.tensor_tensor(rmu_w[:],
                                    mv[:, 0:1].broadcast_to([128, 128]),
                                    rstd_w[:], ALU.mult)
                    S.ve.inc(v.tensor_copy(onemu_g[:, 2 * ci + 1:2 * ci + 2],
                                           rmu_w[:, 0:1]), 1)
                    S.ve.inc(v.tensor_scalar(rmu_w[:], rmu_w[:], -1.0, None,
                                             ALU.mult), 1)
                    S.ve.inc(v.tensor_scalar(xp_g[:, ci * D:(ci + 1) * D],
                                             xtok2[buf][:], rstd_w[:, 0:1],
                                             None, ALU.mult), 1)
                    S.pxf.inc(v.engine_nop(), 1)

                onemu_marks[ch] = S.ve.n

                @block.vector
                def _(v, ch=ch, ci=ci):
                    S.te.wait_at(v, sp_marks[ch])
                    pp = ps_s if ch % 2 == 0 else ps_x2
                    S.ve.inc(v.tensor_scalar(s_st1[:], pp[:], rstd_w[:, 0:1],
                                             None, ALU.mult), 1)
                    S.ve.wait(v)
                    S.ve.inc(v.scalar_tensor_tensor(s_st1[:], csw_bc[:],
                                                    rmu_w[:, 0:1], s_st1[:],
                                                    ALU.mult, ALU.add), 1)
                    S.ve.wait(v)
                    S.ve.inc(v.tensor_scalar(s_bfc[ch % 2][:], s_st1[:],
                                             -CLAMP, CLAMP,
                                             ALU.max, ALU.min), 1)

                te_gate = S.te.n if (ci == 0 and gi > 0) else None
                p1_exp_jobs.append((S.ve.n, ch, ci, te_gate))

            emit_amm(NPRE - 1)
            emit_xaccum(ngrp_pre - 1)

            @block.scalar
            def _(s):
                for m, ch, ci, te_gate in p1_exp_jobs:
                    S.ve.wait_at(s, m)
                    if te_gate is not None:
                        S.te.wait_at(s, te_gate)
                    S.sev.inc(s.activation(e_g[:, ci * MD:(ci + 1) * MD],
                                           s_bfc[ch % 2][:], AF.Exp,
                                           bias=vb_sb[:, ch:ch + 1]), 1)

            @block.vector
            def _(v):
                S.te.wait(v)
                v.tensor_copy(A_row[0:1, 0:512], ps_a1[0:1, 0:512])
                S.ve.inc(v.tensor_copy(A_row[0:1, 512:MD], ps_a2[0:1, 0:512]), 1)
                v.tensor_copy(s_st1[0:2, 0:512], ps_a1[0:2, 0:512])
                S.ve.inc(v.tensor_copy(s_st1[0:2, 512:MD], ps_a2[0:2, 0:512]), 1)

            @block.gpsimd
            def _(g):
                S.ve.wait(g)
                S.ca.inc(g.dma_start(out=crow_sb[0:1, 0:512],
                                     in_=s_st1[1:2, 0:512]), 16)
                S.ca.inc(g.dma_start(out=crow_sb[0:1, 512:MD],
                                     in_=s_st1[1:2, 512:MD]), 16)

            bcast_row(crow_sb[0:1, 0:512], cbc_sb[:, 0:512], ps_s[:, 0:512],
                      extra=(S.ca,))
            bcast_row(crow_sb[0:1, 512:MD], cbc_sb[:, 512:MD], ps_s[:, 0:512])

            @block.vector
            def _(v):
                S.ve.wait(v)
                S.xev.wait(v)
                for dc in range(DCH):
                    v.tensor_tensor(X_sb[:, dc * MD:(dc + 1) * MD],
                                    X_sb[:, dc * MD:(dc + 1) * MD],
                                    cbc_sb[:], ALU.subtract)
                S.ve.inc(v.engine_nop(), 1)
                S.ve.wait(v)
                for dc in range(DCH):
                    ii = v.tensor_copy(X_bf[:, dc * MD:(dc + 1) * MD],
                                       X_sb[:, dc * MD:(dc + 1) * MD])
                S.ve.inc(ii, 1)

            if debug:
                @block.sync
                def _(sy):
                    S.ve.wait(sy)
                    S.st.inc(sy.dma_start(out=d_X, in_=X_sb[:]), 16)
                    S.st.inc(sy.dma_start(out=d_A, in_=A_row[:]), 16)

            barrier()

        xnT = p12.enter_context(nc.sbuf_tensor("xnT", [128, DCH * TC], BF16))
        with contextlib.ExitStack() as p2a:
            sb2 = lambda n, s, d: p2a.enter_context(nc.sbuf_tensor(n, s, d))
            xo2 = [sb2(f"xo{i}", [128, D], F32) for i in range(2)]
            xn2 = [sb2(f"xn{i}", [128, D], BF16) for i in range(2)]
            ps_tp = p2a.enter_context(nc.psum_tensor("ps_tp_a", [128, 2048], BF16))
            tp_rot = Rot(2, S.tpe)

            for ch in range(NOWN):
                buf = ch % 2

                @block.sync
                def _(sy, ch=ch, buf=buf):
                    S.oxf.wait_at(sy, ch - 1)
                    S.ld.inc(sy.dma_start(out=xo2[buf][:],
                                          in_=xown_tok[ch * C:(ch + 1) * C, :]), 16)

                @block.vector
                def _(v, buf=buf):
                    S.ld.wait(v)
                    for j in range(4):
                        ii = v.bn_stats(bnst[:, j * 6:(j + 1) * 6],
                                        xo2[buf][:, j * 512:(j + 1) * 512])
                    S.ve.inc(ii, 1)
                    S.ve.wait(v)
                    S.ve.inc(v.bn_aggr(mv[:], bnst[:]), 1)

                @block.vector
                def _(v, ch=ch, buf=buf):
                    v.tensor_copy(qk_b[:], gnw_bc[:, 0:128])
                    v.tensor_scalar(qk_a[:],
                                    mv[:, 1:2].broadcast_to([128, 128]),
                                    EPS, None, ALU.add)
                    rsqrt_dve(v, rstd_w[:], qk_a[:], qk_b[:], 128)
                    S.ve.inc(v.tensor_scalar(rmu_w[:],
                                             mv[:, 0:1].broadcast_to([128, 128]),
                                             -1.0, None, ALU.mult), 1)
                    S.tpe.wait_at(v, tp_rot.base + (ch - 1) * DCH)
                    S.ve.inc(v.tensor_scalar(xn2[buf][:], xo2[buf][:],
                                             rmu_w[:, 0:1], rstd_w[:, 0:1],
                                             ALU.add, ALU.mult), 1)
                    S.oxf.inc(v.engine_nop(), 1)

                @block.tensor
                def _(t, buf=buf):
                    S.ve.wait(t)
                    for dc in range(DCH):
                        tp_rot.acquire(t)
                        sl = tp_rot.emitted()
                        ii = t.transpose(ps_tp[:, sl * 1024:sl * 1024 + 128],
                                         xn2[buf][:, dc * 128:(dc + 1) * 128],
                                         ident[:])
                        S.te.inc(ii, 1)

                @block.vector
                def _(v, ch=ch):
                    base = S.te.n - DCH
                    for dc in range(DCH):
                        sl = (tp_rot.idx - DCH + dc) % 2
                        S.te.wait_at(v, base + dc + 1)
                        ii = v.tensor_copy(xnT[:, dc * TC + ch * C:
                                               dc * TC + (ch + 1) * C],
                                           ps_tp[:, sl * 1024:sl * 1024 + 128])
                        S.tpe.inc(ii, 1)

            if debug:
                @block.sync
                def _(sy):
                    S.tpe.wait(sy)
                    S.st.inc(sy.dma_start(out=d_xnT, in_=xnT[:]), 16)

        barrier()

        with contextlib.ExitStack() as p2b:
            sb2 = lambda n, s, d: p2b.enter_context(nc.sbuf_tensor(n, s, d))
            wbuf = [sb2(f"wbuf{i}", [128, DCH * 512], BF16) for i in range(2)]
            stg = [sb2(f"stg{i}", [128, 512], BF16) for i in range(4)]
            stg2 = [sb2(f"stg2{i}", [128, 512], BF16) for i in range(2)]
            sclip = sb2("sclip", [128, 512], BF16)
            th_sg = sb2("th_sg", [128, 512], F32)
            ps_pj = p2b.enter_context(nc.psum_tensor("ps_pj_b", [128, 1024], F32))
            ps_tp = p2b.enter_context(nc.psum_tensor("ps_tp_b", [128, 2048], BF16))
            ps_up = p2b.enter_context(nc.psum_tensor("ps_up_b", [128, 1024], F32))
            fcb2 = [sb2(f"fcb2{i}", [128, 2048], F32) for i in range(2)]
            mskb = sb2("mskb", [128, 2048], F32)
            twb2 = sb2("twb2", [128, 2048], BF16)
            wob2 = sb2("wob2", [128, DCH * 128], BF16)
            ampf = sb2("ampf", [128, FF // 128], F32)
            ampp = sb2("ampp", [128, NFT], F32)
            amtmp = sb2("amtmp", [128, 2], F32)
            pj_rot = Rot(2, S.pje)
            tp_rot = Rot(2, S.tpe)
            up_rot = Rot(2, S.upe)
            sg_pool = SlotPool(4, S.sp)
            sg2_pool = SlotPool(2, S.sp)

            def am_task(i):
                wb = i % 2

                @block.sync
                def _(sy, i=i, wb=wb):
                    S.amf.wait_at(sy, i - 1)
                    if i < FF // 128:
                        S.aml.inc(sy.dma_start(
                            out=fcb2[wb].ap().rearrange("p (c f) -> p c f", f=128),
                            in_=wfcT_h[:, i * 128:(i + 1) * 128]
                            .rearrange("(c p) f -> p c f", p=128)), 16)
                    else:
                        j = i - FF // 128
                        S.aml.inc(sy.dma_start(
                            out=fcb2[wb][:],
                            in_=wprojT_h[j * 128:(j + 1) * 128, :]), 16)

                @block.vector
                def _(v, i=i, wb=wb):
                    S.aml.wait(v)
                    dst = (ampf[:, i:i + 1] if i < FF // 128
                           else ampp[:, i - FF // 128:i - FF // 128 + 1])
                    ii = v.tensor_reduce(dst, fcb2[wb][:], AX.X, ALU.add,
                                         apply_absolute_value=True)
                    S.amf.inc(ii, 1)

            def fin_task():
                @block.vector
                def _(v):
                    S.amf.wait(v)
                    S.fzv.inc(v.tensor_reduce(amtmp[:, 0:1], ampf[:], AX.X,
                                              ALU.add), 1)
                    S.fzv.inc(v.tensor_reduce(amtmp[:, 1:2], ampp[:], AX.X,
                                              ALU.add), 1)

                @block.tensor
                def _(t):
                    S.fzv.wait(t)
                    S.fzt.inc(t.matmul(ps_up[0:1, 0:1], amtmp[:, 0:1],
                                       onesc_f32[:], start=True, stop=True,
                                       skip_group_check=True), 1)
                    S.fzt.inc(t.matmul(ps_up[0:1, 1:2], amtmp[:, 1:2],
                                       onesc_f32[:], start=True, stop=True,
                                       skip_group_check=True), 1)

                @block.vector
                def _(v):
                    S.fzt.wait(v)
                    S.fzv.inc(v.tensor_scalar(srow[0:1, 0:1], ps_up[0:1, 0:1],
                                              1.0 / (FF * D), 1e-5, ALU.mult,
                                              ALU.max), 1)
                    S.fzv.inc(v.tensor_scalar(srow[0:1, 1:2], ps_up[0:1, 1:2],
                                              1.0 / (FF * D), 1e-5, ALU.mult,
                                              ALU.max), 1)
                    S.fzv.wait(v)
                    S.fzv.inc(v.tensor_scalar(srow[0:1, 3:4], srow[0:1, 0:1],
                                              0.7, None, ALU.mult), 1)
                    S.fzv.inc(v.tensor_scalar(srow[0:1, 4:5], srow[0:1, 1:2],
                                              0.7, None, ALU.mult), 1)
                    S.fzv.wait(v)
                    S.fzv.inc(v.tensor_scalar(srow[0:1, 5:6], srow[0:1, 3:4],
                                              -1.0, None, ALU.mult), 1)
                    S.fzv.inc(v.tensor_scalar(srow[0:1, 6:7], srow[0:1, 4:5],
                                              -1.0, None, ALU.mult), 1)
                    S.fzv.inc(v.tensor_tensor(srow[0:1, 2:3], srow[0:1, 0:1],
                                              srow[0:1, 0:1], ALU.mult), 1)
                    S.fzv.wait(v)
                    S.fzv.inc(v.tensor_tensor(srow[0:1, 2:3], srow[0:1, 2:3],
                                              srow[0:1, 1:2], ALU.mult), 1)

                rows = [(srow[0:1, 3:4], thr_col[:, 0:1]),
                        (srow[0:1, 4:5], thr_col[:, 1:2]),
                        (srow[0:1, 5:6], nthr_col[:, 0:1]),
                        (srow[0:1, 6:7], nthr_col[:, 1:2]),
                        (srow[0:1, 2:3], amc_col[:, 0:1])]
                for row_ap, out_ap in rows:
                    @block.tensor
                    def _(t, row_ap=row_ap):
                        S.fzv.wait(t)
                        S.fzt.inc(t.matmul(ps_up[:, 2:3], ones1_f32[:], row_ap,
                                           start=True, stop=True,
                                           skip_group_check=True), 1)

                    @block.vector
                    def _(v, out_ap=out_ap):
                        S.fzt.wait(v)
                        ii = v.tensor_copy(out_ap, ps_up[:, 2:3])
                        S.fzv.inc(ii, 1)
                        S.fz.inc(v.engine_nop(), 1)

            def tern_task(j):
                wb = j % 2

                @block.sync
                def _(sy, j=j, wb=wb):
                    S.fz.wait_at(sy, 4)
                    S.wte2.wait_at(sy, j - 1)
                    S.tld.inc(sy.dma_start(
                        out=fcb2[wb][:],
                        in_=wprojT_h[j * 128:(j + 1) * 128, :]), 16)

                @block.vector
                def _(v, j=j, wb=wb):
                    S.tld.wait(v)
                    S.wte2.wait_at(v, j)
                    S.fzv.inc(v.tensor_scalar(mskb[:], fcb2[wb][:],
                                              nthr_col[:, 1:2], thr_col[:, 1:2],
                                              ALU.max, ALU.min), 1)
                    S.fzv.wait(v)
                    S.fzv.inc(v.tensor_tensor(mskb[:], fcb2[wb][:], mskb[:],
                                              ALU.subtract), 1)

                @block.scalar
                def _(s, j=j):
                    S.fzv.wait(s)
                    S.wsp2.wait_at(s, j * 16)
                    s.activation(twb2[:], mskb[:], AF.Sign)
                    S.wte2.inc(s.drain(), 1)

                @block.gpsimd
                def _(g, j=j):
                    S.wte2.wait(g)
                    ii = g.dma_start(
                        out=d_wpt[:, :, j * 128:(j + 1) * 128]
                        .rearrange("d p f -> p d f"),
                        in_=twb2.ap().rearrange("p (d f) -> p d f", f=128))
                    S.wsp2.inc(ii, 16)

            def wo_task(k):
                @block.gpsimd
                def _(g, k=k):
                    S.woe.wait(g)
                    S.ca.inc(g.dma_start(
                        out=wob2.ap().rearrange("p (c f) -> p c f", f=128),
                        in_=wo_h[:, k * 128:(k + 1) * 128]
                        .rearrange("(c p) f -> p c f", p=128)), 16)
                    g.wait_ge(S.ca.h, S.ca.n)
                    ii = g.dma_start(out=d_wo[k], in_=wob2[:])
                    S.woe.inc(ii, 16)

            tasks = ([lambda i=i: am_task(i) for i in range(FF // 128 + NFT)]
                     + [fin_task]
                     + [lambda j=j: tern_task(j) for j in range(NFT)]
                     + [lambda k=k: wo_task(k) for k in range(DCH)])
            task_iter = iter(tasks)

            def filler():
                try:
                    next(task_iter)()
                except StopIteration:
                    pass

            def w_load(g, w_ap, fgi, wb):
                S.wuse.wait_at(g, S.wuse.n - 1)
                S.ca.inc(g.dma_start(
                    out=wbuf[wb].ap().rearrange("p (c f) -> p c f", f=512),
                    in_=w_ap[:, fgi * 512:(fgi + 1) * 512]
                    .rearrange("(c p) f -> p c f", p=128)), 16)

            def spill_feat(dram, src_sb, gft, tt4, done_sem):
                ii_holder = []

                @block.gpsimd
                def _(g):
                    done_sem.wait(g)
                    ii = g.dma_start(
                        out=dram[tt4 * 4:(tt4 + 1) * 4, :,
                                 gft * 128:(gft + 1) * 128]
                        .rearrange("c p f -> p c f"),
                        in_=src_sb.ap().rearrange("p (c f) -> p c f", f=128))
                    S.sp.inc(ii, 16)
                    ii_holder.append(ii)
                return ii_holder

            def post_plain(dram):
                def post(gft, tt4, ps_ap, mm_mark):
                    @block.vector
                    def _(v):
                        S.te.wait_at(v, mm_mark)
                        ssl = sg_pool.acquire(v)
                        ii = v.tensor_copy(stg[ssl][:], ps_ap)
                        S.ve.inc(ii, 1)
                        S.pje.inc(v.engine_nop(), 1)
                        post.ssl = ssl

                    @block.gpsimd
                    def _(g, gft=gft, tt4=tt4):
                        S.ve.wait(g)
                        ii = g.dma_start(
                            out=dram[tt4 * 4:(tt4 + 1) * 4, :,
                                     gft * 128:(gft + 1) * 128]
                            .rearrange("c p f -> p c f"),
                            in_=stg[post.ssl].ap()
                            .rearrange("p (c f) -> p c f", f=128))
                        S.sp.inc(ii, 16)
                        sg_pool.release(post.ssl)
                return post

            def transposes_to(src_sb, dram, gft, tt4, ready):
                @block.tensor
                def _(t):
                    ready.wait(t)
                    for q in range(4):
                        tp_rot.acquire(t)
                        sl = tp_rot.emitted()
                        ii = t.transpose(ps_tp[:, sl * 1024:sl * 1024 + 128],
                                         src_sb[:, q * 128:(q + 1) * 128],
                                         ident[:])
                        S.te.inc(ii, 1)

                @block.vector
                def _(v):
                    base = S.te.n - 4
                    s2l = sg2_pool.acquire(v)
                    for q in range(4):
                        sl = (tp_rot.idx - 4 + q) % 2
                        S.te.wait_at(v, base + q + 1)
                        ii = v.tensor_copy(stg2[s2l][:, q * 128:(q + 1) * 128],
                                           ps_tp[:, sl * 1024:sl * 1024 + 128])
                        if q < 3:
                            S.tpe.inc(ii, 1)
                    S.ve.inc(ii, 1)
                    S.tpe.inc(v.engine_nop(), 1)
                    transposes_to.s2l = s2l

                @block.gpsimd
                def _(g, gft=gft, tt4=tt4):
                    S.ve.wait(g)
                    ii = g.dma_start(
                        out=dram[tt4 * 4:(tt4 + 1) * 4, :,
                                 gft * 128:(gft + 1) * 128]
                        .rearrange("c p f -> p c f"),
                        in_=stg2[transposes_to.s2l].ap()
                        .rearrange("p (c f) -> p c f", f=128))
                    S.sp.inc(ii, 16)
                    sg2_pool.release(transposes_to.s2l)

            def post_k(gft, tt4, ps_ap, mm_mark):
                post_plain(d_kT)(gft, tt4, ps_ap, mm_mark)

            def post_s(gft, tt4, ps_ap, mm_mark):
                @block.vector
                def _(v):
                    S.te.wait_at(v, mm_mark)
                    S.sev.wait(v, lag=0)
                    ii = v.tensor_scalar(sclip[:], ps_ap, -CLAMP, CLAMP,
                                         ALU.max, ALU.min)
                    S.ve.inc(ii, 1)
                    S.pje.inc(v.engine_nop(), 1)

                @block.scalar
                def _(s):
                    S.ve.wait(s)
                    ssl = sg_pool.acquire(s)
                    ii = s.activation(stg[ssl][:], sclip[:], AF.Exp)
                    S.sev.inc(ii, 1)
                    post_s.ssl = ssl

                @block.gpsimd
                def _(g, gft=gft, tt4=tt4):
                    S.sev.wait(g)
                    ii = g.dma_start(
                        out=d_ef[tt4 * 4:(tt4 + 1) * 4, :,
                                 gft * 128:(gft + 1) * 128]
                        .rearrange("c p f -> p c f"),
                        in_=stg[post_s.ssl].ap()
                        .rearrange("p (c f) -> p c f", f=128))
                    S.sp.inc(ii, 16)

                transposes_to(stg[post_s.ssl][:], d_et, gft, tt4, S.sev)

                @block.gpsimd
                def _(g):
                    S.te.wait(g)
                    S.sp.wait(g)
                    sg_pool.release(post_s.ssl)
                    S.sge.inc(g.engine_nop(), 1)

            def sweep_feat(w_ap, FW, post):
                for fgi in range(FW // 512):
                    wb = fgi % 2

                    @block.gpsimd
                    def _(g, fgi=fgi, wb=wb):
                        w_load(g, w_ap, fgi, wb)

                    for tt4 in range(NT4):
                        for ft in range(4):
                            gft = fgi * 4 + ft

                            @block.tensor
                            def _(t, wb=wb, tt4=tt4, ft=ft):
                                S.ca.wait(t)
                                pj_rot.acquire(t)
                                sl = pj_rot.emitted()
                                for dc in range(DCH):
                                    ii = t.matmul(
                                        ps_pj[:, sl * 512:(sl + 1) * 512],
                                        wbuf[wb][:, dc * 512 + ft * 128:
                                                 dc * 512 + (ft + 1) * 128],
                                        xnT[:, dc * TC + tt4 * 512:
                                            dc * TC + (tt4 + 1) * 512],
                                        start=(dc == 0), stop=(dc == DCH - 1))
                                S.te.inc(ii, 1)
                                if tt4 == NT4 - 1 and ft == 3:
                                    S.wuse.inc(t.drain(), 1)

                            sl = (pj_rot.idx - 1) % 2
                            post(gft, tt4,
                                 ps_pj[:, sl * 512:(sl + 1) * 512], S.te.n)
                            filler()

            def sweep_tok(w_ap, FW, dram, silu):
                sweep_tok.thmark = None
                for fgi in range(FW // 512):
                    wb = fgi % 2

                    @block.gpsimd
                    def _(g, fgi=fgi, wb=wb):
                        w_load(g, w_ap, fgi, wb)

                    for ch in range(NOWN):
                        @block.tensor
                        def _(t, wb=wb, ch=ch):
                            S.ca.wait(t)
                            pj_rot.acquire(t)
                            sl = pj_rot.emitted()
                            for dc in range(DCH):
                                ii = t.matmul(
                                    ps_pj[:, sl * 512:(sl + 1) * 512],
                                    xnT[:, dc * TC + ch * C:dc * TC + (ch + 1) * C],
                                    wbuf[wb][:, dc * 512:(dc + 1) * 512],
                                    start=(dc == 0), stop=(dc == DCH - 1))
                            S.te.inc(ii, 1)
                            if ch == NOWN - 1:
                                S.wuse.inc(t.drain(), 1)

                        sl = (pj_rot.idx - 1) % 2
                        mm_mark = S.te.n
                        if silu:
                            @block.scalar
                            def _(s, ch=ch, sl=sl, mm_mark=mm_mark):
                                S.te.wait_at(s, mm_mark)
                                if sweep_tok.thmark is not None:
                                    S.th.wait_at(s, sweep_tok.thmark)
                                ssl = sg_pool.acquire(s)
                                ii = s.activation(
                                    th_sg[:],
                                    ps_pj[:, sl * 512:(sl + 1) * 512],
                                    AF.Tanh, scale=0.5)
                                S.sle.inc(ii, 1)
                                sweep_tok.ssl = ssl

                            @block.vector
                            def _(v, sl=sl):
                                S.sle.wait(v)
                                ii = v.scalar_tensor_tensor(
                                    stg[sweep_tok.ssl][:], th_sg[:], 1.0,
                                    ps_pj[:, sl * 512:(sl + 1) * 512],
                                    ALU.add, ALU.mult)
                                S.th.inc(ii, 1)
                                sweep_tok.thmark = S.th.n
                                S.pje.inc(v.engine_nop(), 1)

                            @block.gpsimd
                            def _(g, fgi=fgi, ch=ch):
                                S.th.wait(g)
                                ii = g.dma_start(
                                    out=dram[ch, :, fgi * 512:(fgi + 1) * 512],
                                    in_=stg[sweep_tok.ssl][:])
                                S.sp.inc(ii, 16)
                                sg_pool.release(sweep_tok.ssl)
                        else:
                            @block.vector
                            def _(v, ch=ch, sl=sl, mm_mark=mm_mark):
                                S.te.wait_at(v, mm_mark)
                                ssl = sg_pool.acquire(v)
                                ii = v.tensor_copy(
                                    stg[ssl][:],
                                    ps_pj[:, sl * 512:(sl + 1) * 512])
                                S.ve.inc(ii, 1)
                                S.pje.inc(v.engine_nop(), 1)
                                sweep_tok.ssl = ssl

                            @block.gpsimd
                            def _(g, fgi=fgi, ch=ch):
                                S.ve.wait(g)
                                ii = g.dma_start(
                                    out=dram[ch, :, fgi * 512:(fgi + 1) * 512],
                                    in_=stg[sweep_tok.ssl][:])
                                S.sp.inc(ii, 16)
                                sg_pool.release(sweep_tok.ssl)
                        filler()

            def sk0_from_x(fgi, wb):
                for j in range(8):
                    h = fgi * 8 + j

                    @block.tensor
                    def _(t, h=h, j=j, wb=wb):
                        S.ca.wait(t)
                        up_rot.acquire(t)
                        usl = up_rot.emitted()
                        for dc in range(DCH):
                            ii = t.matmul(
                                ps_up[0:M, usl * 512:usl * 512 + M],
                                wbuf[wb][:, dc * 512 + j * 64:
                                         dc * 512 + (j + 1) * 64],
                                X_bf[:, dc * MD + h * M:dc * MD + (h + 1) * M],
                                start=(dc == 0), stop=(dc == DCH - 1),
                                skip_group_check=True)
                        S.te.inc(ii, 1)
                        sk0_from_x.usl = usl

                    @block.vector
                    def _(v, h=h):
                        usl = sk0_from_x.usl
                        S.te.wait(v)
                        pb = (h % 2) * 64
                        ii = v.tensor_copy(
                            SKs[pb:pb + M, (h // 2) * M:(h // 2 + 1) * M],
                            ps_up[0:M, usl * 512:usl * 512 + M])
                        S.upe.inc(ii, 1)
                        S.ve.inc(v.engine_nop(), 1)

            def sv0_from_x(fgi, wb):
                for j in range(4):
                    h = fgi * 4 + j

                    @block.tensor
                    def _(t, h=h, j=j, wb=wb):
                        S.ca.wait(t)
                        up_rot.acquire(t)
                        usl = up_rot.emitted()
                        for dc in range(DCH):
                            ii = t.matmul(
                                ps_up[0:M, usl * 512:usl * 512 + DV],
                                X_bf[:, dc * MD + h * M:dc * MD + (h + 1) * M],
                                wbuf[wb][:, dc * 512 + j * 128:
                                         dc * 512 + (j + 1) * 128],
                                start=(dc == 0), stop=(dc == DCH - 1),
                                skip_group_check=True)
                        S.te.inc(ii, 1)
                        sv0_from_x.usl = usl

                    @block.vector
                    def _(v, h=h):
                        usl = sv0_from_x.usl
                        S.te.wait(v)
                        pb = (h % 2) * 64
                        ii = v.tensor_copy(
                            SVs[pb:pb + M, (h // 2) * DV:(h // 2 + 1) * DV],
                            ps_up[0:M, usl * 512:usl * 512 + DV])
                        S.upe.inc(ii, 1)
                        S.ve.inc(v.engine_nop(), 1)

            sweep_feat(wk_h, KD, post_k)
            sweep_feat(ws_h, MD, post_s)
            sweep_feat(wq_h, KD, post_plain(d_q))
            sweep_tok(wv_h, VD, d_v, False)
            sweep_tok(wg_h, VD, d_sg, True)
            for _t in task_iter:
                _t()

            @block.scalar
            def _(s):
                act_set(s, 6)
            barrier()

            for fgi in range(KD // 512):
                wb = fgi % 2

                @block.gpsimd
                def _(g, fgi=fgi, wb=wb):
                    S.upe.wait(g)
                    w_load(g, wk_h, fgi, wb)
                sk0_from_x(fgi, wb)
            for fgi in range(VD // 512):
                wb = fgi % 2

                @block.gpsimd
                def _(g, fgi=fgi, wb=wb):
                    S.upe.wait(g)
                    w_load(g, wv_h, fgi, wb)
                sv0_from_x(fgi, wb)

            if debug:
                @block.sync
                def _(sy):
                    S.ve.wait(sy)
                    S.st.inc(sy.dma_start(out=d_SK, in_=SKs[:]), 16)
                    S.st.inc(sy.dma_start(out=d_SV, in_=SVs[:]), 16)

        barrier()

        with contextlib.ExitStack() as p2c:
            sb2 = lambda n, s, d: p2c.enter_context(nc.sbuf_tensor(n, s, d))
            fcb = [sb2(f"fcb{i}", [128, DCH * 256], F32) for i in range(2)]
            msk = sb2("msk", [128, DCH * 256], F32)
            twb = sb2("twb", [128, DCH * 256], BF16)
            rl_st = [sb2(f"rl{i}", [128, 512], BF16) for i in range(2)]
            hr_st = [sb2(f"hr{i}", [128, 512], BF16) for i in range(2)]
            ps_pj = p2c.enter_context(nc.psum_tensor("ps_pj_c", [128, 1024], F32))
            hr_rot = Rot(2, S.hre)
            sev_rot = Rot(2, S.sev)
            NSL_FC = FF // 256

            for fs in range(NSL_FC):
                wb = fs % 2

                @block.sync
                def _(sy, fs=fs, wb=wb):
                    S.twe.wait_at(sy, fs - 1)
                    S.ld.inc(sy.dma_start(
                        out=fcb[wb].ap().rearrange("p (c f) -> p c f", f=256),
                        in_=wfcT_h[:, fs * 256:(fs + 1) * 256]
                        .rearrange("(c p) f -> p c f", p=128)), 16)

                @block.vector
                def _(v, fs=fs, wb=wb):
                    S.ld.wait(v)
                    S.twe.wait(v)
                    S.ve.inc(v.tensor_scalar(msk[:], fcb[wb][:],
                                             nthr_col[:, 0:1],
                                             thr_col[:, 0:1], ALU.max,
                                             ALU.min), 1)
                    S.ve.wait(v)
                    S.ve.inc(v.tensor_tensor(msk[:], fcb[wb][:], msk[:],
                                             ALU.subtract), 1)

                @block.scalar
                def _(s):
                    S.ve.wait(s)
                    S.te.wait(s)
                    s.activation(twb[:], msk[:], AF.Sign)
                    S.twe.inc(s.drain(), 1)

                for tt4 in range(NT4):
                    for ft in range(2):
                        gft = fs * 2 + ft

                        @block.tensor
                        def _(t, tt4=tt4, ft=ft):
                            S.twe.wait(t)
                            sev_rot.acquire(t)
                            sl = sev_rot.emitted()
                            for dc in range(DCH):
                                ii = t.matmul(
                                    ps_pj[:, sl * 512:(sl + 1) * 512],
                                    twb[:, dc * 256 + ft * 128:
                                        dc * 256 + (ft + 1) * 128],
                                    xnT[:, dc * TC + tt4 * 512:
                                        dc * TC + (tt4 + 1) * 512],
                                    start=(dc == 0), stop=(dc == DCH - 1))
                            S.te.inc(ii, 1)

                        sl = (sev_rot.idx - 1) % 2
                        mm_mark = S.te.n

                        @block.scalar
                        def _(s, sl=sl, mm_mark=mm_mark):
                            S.te.wait_at(s, mm_mark)
                            hr_rot.acquire(s)
                            hsl = hr_rot.emitted()
                            ii = s.activation(rl_st[hsl][:],
                                              ps_pj[:, sl * 512:(sl + 1) * 512],
                                              AF.Relu)
                            S.sev.inc(ii, 1)

                        hsl_now = (hr_rot.idx - 1) % 2

                        @block.vector
                        def _(v, hsl=hsl_now):
                            S.sev.wait(v)
                            S.sp16.wait_at(v, S.sp16.n - 16)
                            ii = v.tensor_tensor(hr_st[hsl][:], rl_st[hsl][:],
                                                 rl_st[hsl][:], ALU.mult)
                            S.hre.inc(ii, 1)

                        @block.gpsimd
                        def _(g, gft=gft, tt4=tt4, hsl=hsl_now):
                            S.hre.wait(g)
                            ii = g.dma_start(
                                out=d_hr[gft, :, tt4 * 512:(tt4 + 1) * 512],
                                in_=hr_st[hsl][:])
                            S.sp16.inc(ii, 16)

        barrier()

        p12.close()
        barrier()

        with contextlib.ExitStack() as p3:
            sb3 = lambda n, s, d: p3.enter_context(nc.sbuf_tensor(n, s, d))
            rq = [sb3(f"rq{i}", [128, KD], BF16) for i in range(2)]
            rkT = [sb3(f"rkT{i}", [128, KD], BF16) for i in range(2)]
            ktok = sb3("ktok", [128, KD], BF16)
            ref_ = [sb3(f"ref{i}", [128, MD], BF16) for i in range(2)]
            ret = [sb3(f"ret{i}", [128, MD], BF16) for i in range(2)]
            rv = [sb3(f"rv{i}", [128, VD], BF16) for i in range(2)]
            rsg = [sb3(f"rsg{i}", [128, VD], BF16) for i in range(2)]
            hr_g = sb3("hr_g", [128, NFT * 512], BF16)
            wptb = [sb3(f"wptb{i}", [128, NFT * 128], BF16) for i in range(2)]
            wobuf = [sb3(f"wobuf{i}", [128, DCH * 128], BF16) for i in range(2)]
            xat = sb3("xat", [128, DCH * 512], F32)
            ogT_g = sb3("ogT_g", [128, DCH * 512], BF16)
            a_sb = [sb3(f"a_sb{i}", [128, MD], F32) for i in range(2)]
            okr = [sb3(f"okr{i}", [128, MD], F32) for i in range(2)]
            rA = sb3("rA", [128, MD], F32)
            u_bf = sb3("u_bf", [128, MD], BF16)
            ut_bf = sb3("ut_bf", [128, 8 * 128], BF16)
            og_bf = sb3("og_bf", [128, VD], BF16)
            ov_sb = sb3("ov_sb", [128, VD], BF16)
            ptm = [sb3(f"ptm{i}", [128, 128], BF16) for i in range(2)]
            w2m = [sb3(f"w2m{i}", [128, 128], BF16) for i in range(2)]
            t1g = sb3("t1g", [128, DV], F32)
            mx16 = sb3("mx16", [128, 16], F32)
            nmx16 = sb3("nmx16", [128, 16], F32)
            sume = sb3("sume", [128, 16], F32)
            rsum = sb3("rsum", [128, 16], F32)
            ssq = sb3("ssq", [128, 64], F32)
            rstd_o = sb3("rstd_o", [128, 64], F32)
            skb3 = [sb3(f"skb3{i}", [128, 8 * M], BF16) for i in range(3)]
            svb3 = [sb3(f"svb3{i}", [128, 8 * DV], BF16) for i in range(3)]

            ps_ok = p3.enter_context(nc.psum_tensor("ps_ok", [128, 1024], F32))
            ps_ai = p3.enter_context(nc.psum_tensor("ps_ai", [128, 1024], F32))
            ps_pt = p3.enter_context(nc.psum_tensor("ps_pt", [128, 1024], F32))
            ps_tp = p3.enter_context(nc.psum_tensor("ps_tp_3", [128, 2048], BF16))
            ps_ov = ps_pt
            tp_rot = Rot(2, S.tpe)
            marks = {"ut": 0, "og": 0, "rA": {}, "okn": {}}

            @block.sync
            def _(sy):
                S.sp.wait(sy)
                S.sp16.wait(sy)
                S.woe.wait(sy)
                S.wsp2.wait(sy)
                S.spg.inc(sy.nop(), 1)

            @block.gpsimd
            def _(g):
                S.spg.wait(g)

            @block.vector
            def _(v):
                v.memset(ssq[:], 1.0)

            def ring_load(ch):
                sl = ch % 2

                @block.sync
                def _(sy, ch=ch, sl=sl):
                    S.gate.wait_at(sy, (ch - 1) * H)
                    for dram, buf in ((d_q, rq), (d_kT, rkT),
                                      (d_ef, ref_), (d_et, ret),
                                      (d_v, rv), (d_sg, rsg)):
                        S.rld.inc(sy.dma_start(out=buf[sl][:], in_=dram[ch]), 16)

            def fslice(buf, h, w):
                pb = (h % 2) * 64
                return buf[pb:pb + 64, (h // 2) * w:(h // 2) * w + w]

            def phase1(ch):
                sl = ch % 2
                q, kT, et, v_ = rq[sl], rkT[sl], ret[sl], rv[sl]
                kt = ktok
                SKb_c = skb3[ch % 3]

                te_kt_base = S.te.n

                @block.tensor
                def _(t):
                    S.rld.wait(t)
                    for j in range(8):
                        tp_rot.acquire(t)
                        tsl = tp_rot.emitted()
                        ii = t.transpose(ps_tp[:, tsl * 1024:tsl * 1024 + 128],
                                         kT[:, j * 128:(j + 1) * 128], ident[:])
                        S.te.inc(ii, 1)

                @block.vector
                def _(v):
                    for j in range(8):
                        tsl = (tp_rot.idx - 8 + j) % 2
                        S.te.wait_at(v, te_kt_base + j + 1)
                        ii = v.tensor_copy(ktok[:, j * 128:(j + 1) * 128],
                                           ps_tp[:, tsl * 1024:tsl * 1024 + 128])
                        S.tpe.inc(ii, 1)

                @block.tensor
                def _(t):
                    S.stup.wait(t)
                    S.stu.wait(t)
                    S.te.inc(t.matmul(ps_ai[:, 0:512], ones1_f32[:],
                                      A_row[0:1, 0:512], start=True, stop=False,
                                      skip_group_check=True), 1)
                    S.te.inc(t.matmul(ps_ai[:, 512:1024], ones1_f32[:],
                                      A_row[0:1, 512:MD], start=True, stop=False,
                                      skip_group_check=True), 1)

                te_pt_base = S.te.n
                ptm_base = S.ptme.n

                @block.tensor
                def _(t):
                    S.rld.wait(t)
                    S.ove.wait(t)
                    for h in range(H):
                        if h >= 2:
                            S.ptme.wait_at(t, ptm_base + h - 1)
                        psl = h % 2
                        ii = t.matmul(ps_pt[:, psl * 512:psl * 512 + 128],
                                      fslice(kT, h, 128), fslice(q, h, 128),
                                      start=True, stop=True,
                                      skip_group_check=True)
                        S.te.inc(ii, 1)

                @block.vector
                def _(v):
                    for h in range(H):
                        psl = h % 2
                        S.te.wait_at(v, te_pt_base + h + 1)
                        ii = v.tensor_tensor(ptm[psl][:],
                                             ps_pt[:, psl * 512:psl * 512 + 128],
                                             mask01[:], ALU.mult)
                        S.ptme.inc(ii, 1)

                @block.tensor
                def _(t, ch=ch):
                    S.stc.wait(t)
                    S.okf.wait(t)
                    for h in range(H):
                        S.ptme.wait_at(t, ptm_base + h + 1)
                        psl = h % 2
                        t.matmul(ps_ok[:, h * 64:(h + 1) * 64], ptm[psl][:],
                                 et[:, h * 64:(h + 1) * 64], start=True,
                                 stop=False, skip_group_check=True)
                        ii = t.matmul(ps_ok[:, h * 64:(h + 1) * 64],
                                      fslice(q, h, 128), fslice(SKb_c, h, 64),
                                      start=False, stop=True,
                                      skip_group_check=True)
                        S.te.inc(ii, 1)
                    for h in range(H):
                        ii = t.matmul(ps_ai[:, h * 64:(h + 1) * 64], mask01[:],
                                      et[:, h * 64:(h + 1) * 64], start=False,
                                      stop=(h % 8 == 7), skip_group_check=True)
                    S.te.inc(ii, 1)

                @block.vector
                def _(v, ch=ch):
                    S.te.wait(v)
                    ii = v.tensor_copy(okr[ch % 2][:], ps_ok[:])
                    S.okf.inc(ii, 1)
                    ii = v.tensor_copy(a_sb[ch % 2][:], ps_ai[:])
                    S.ve.inc(ii, 1)
                    S.aie.inc(v.engine_nop(), 1)

                stup_base = S.stup.n

                @block.tensor
                def _(t, ch=ch):
                    S.aie.wait(t)
                    for h in range(H):
                        usl = h % 2
                        S.stup.wait_at(t, stup_base + h - 1)
                        t.matmul(ps_ai[0:64, usl * 512 + 0:usl * 512 + 64],
                                 kt[:, h * 64:(h + 1) * 64],
                                 et[:, h * 64:(h + 1) * 64],
                                 start=True, stop=True, skip_group_check=True)
                        t.matmul(ps_ai[0:64, usl * 512 + 128:usl * 512 + 256],
                                 et[:, h * 64:(h + 1) * 64],
                                 v_[:, h * 128:(h + 1) * 128],
                                 start=True, stop=True, skip_group_check=True)
                        ii = t.matmul(ps_ai[0:1, usl * 512 + 64:usl * 512 + 128],
                                      ones_bf[:, 0:1],
                                      et[:, h * 64:(h + 1) * 64],
                                      start=True, stop=True,
                                      skip_group_check=True)
                        S.te.inc(ii, 1)

                @block.vector
                def _(v, ch=ch):
                    base = S.te.n - H
                    for h in range(H):
                        usl = h % 2
                        S.te.wait_at(v, base + h + 1)
                        pb = (h % 2) * 64
                        v.tensor_tensor(
                            SKs[pb:pb + 64, (h // 2) * M:(h // 2 + 1) * M],
                            SKs[pb:pb + 64, (h // 2) * M:(h // 2 + 1) * M],
                            ps_ai[0:64, usl * 512 + 0:usl * 512 + 64], ALU.add)
                        v.tensor_tensor(
                            SVs[pb:pb + 64, (h // 2) * DV:(h // 2 + 1) * DV],
                            SVs[pb:pb + 64, (h // 2) * DV:(h // 2 + 1) * DV],
                            ps_ai[0:64, usl * 512 + 128:usl * 512 + 256],
                            ALU.add)
                        ii = v.tensor_tensor(
                            A_row[0:1, h * 64:(h + 1) * 64],
                            A_row[0:1, h * 64:(h + 1) * 64],
                            ps_ai[0:1, usl * 512 + 64:usl * 512 + 128], ALU.add)
                        S.stup.inc(ii, 1)
                    S.stu.inc(v.engine_nop(), 1)
                    S.ve.wait(v)
                    S.ove.wait(v)
                    v.tensor_copy(skb3[(ch + 1) % 3][:], SKs[:])
                    ii = v.tensor_copy(svb3[(ch + 1) % 3][:], SVs[:])
                    S.stc.inc(ii, 1)

            def phase2(ch, g):
                sl = ch % 2
                q, kT, ef, v_, sg = rq[sl], rkT[sl], ref_[sl], rv[sl], rsg[sl]
                SVb_c = svb3[ch % 3]

                @block.vector
                def _(v, ch=ch):
                    S.se.wait_at(v, se_b3 + ch)
                    ok_ = okr[ch % 2]
                    v.reciprocal(rA[:], a_sb[ch % 2][:])
                    v.tensor_tensor(ok_[:, 0:512], ok_[:, 0:512],
                                    rA[:, 0:512], ALU.mult)
                    ii = v.tensor_tensor(ok_[:, 512:MD], ok_[:, 512:MD],
                                         rA[:, 512:MD], ALU.mult)
                    S.ve.inc(ii, 1)
                    S.ve.wait(v)
                    S.ve.inc(v.tensor_reduce(
                        mx16[:], ok_.ap().rearrange("p (h m) -> p h m", m=64),
                        AX.X, ALU.max), 1)
                    S.ve.wait(v)
                    S.ve.inc(v.tensor_scalar(nmx16[:], mx16[:], -SCALE, None,
                                             ALU.mult), 1)

                p3_sc_jobs.append(("exps", S.ve.n, ch))

                @block.vector
                def _(v, ch=ch):
                    S.se.wait_at(v, se_b3 + ch + 1)
                    ok_ = okr[ch % 2]
                    S.ve.inc(v.reciprocal(rsum[:], sume[:]), 1)
                    S.ve.wait(v)
                    S.tpe.wait_at(v, marks["ut"])
                    for h in range(H):
                        ii = v.scalar_tensor_tensor(
                            u_bf[:, h * 64:(h + 1) * 64],
                            ok_[:, h * 64:(h + 1) * 64], rsum[:, h:h + 1],
                            rA[:, h * 64:(h + 1) * 64], ALU.mult, ALU.mult)
                    S.ve.inc(ii, 1)

                te_ut_base = S.te.n

                @block.tensor
                def _(t):
                    S.ve.wait(t)
                    for j in range(8):
                        tp_rot.acquire(t)
                        tsl = tp_rot.emitted()
                        ii = t.transpose(ps_tp[:, tsl * 1024:tsl * 1024 + 128],
                                         u_bf[:, j * 128:(j + 1) * 128], ident[:])
                        S.te.inc(ii, 1)

                @block.vector
                def _(v):
                    S.ove.wait(v)
                    for j in range(8):
                        tsl = (tp_rot.idx - 8 + j) % 2
                        S.te.wait_at(v, te_ut_base + j + 1)
                        ii = v.tensor_copy(ut_bf[:, j * 128:(j + 1) * 128],
                                           ps_tp[:, tsl * 1024:tsl * 1024 + 128])
                        S.tpe.inc(ii, 1)
                    marks["ut"] = S.tpe.n

                te_w2_base = S.te.n
                w2m_base = S.w2me.n

                @block.tensor
                def _(t):
                    S.ve.wait(t)
                    S.ptme.wait(t)
                    S.ove.wait(t)
                    for h in range(H):
                        if h >= 2:
                            S.w2me.wait_at(t, w2m_base + h - 1)
                        psl = h % 2
                        ii = t.matmul(
                            ps_pt[:, psl * 512:psl * 512 + 128],
                            fslice(ef, h, 128),
                            ut_bf[(h % 2) * 64:(h % 2) * 64 + 64,
                                  (h // 2) * 128:(h // 2 + 1) * 128],
                            start=True, stop=True, skip_group_check=True)
                        S.te.inc(ii, 1)

                @block.vector
                def _(v):
                    for h in range(H):
                        psl = h % 2
                        S.te.wait_at(v, te_w2_base + h + 1)
                        ii = v.tensor_tensor(w2m[psl][:],
                                             ps_pt[:, psl * 512:psl * 512 + 128],
                                             mask01[:], ALU.mult)
                        S.w2me.inc(ii, 1)

                ove_base = S.ove.n
                te_ov_base = S.te.n

                @block.tensor
                def _(t):
                    S.w2me.wait(t)
                    for h in range(H):
                        if h >= 2:
                            S.ove.wait_at(t, ove_base + h - 1)
                        psl = h % 2
                        t.matmul(ps_ov[:, psl * 512:psl * 512 + 128],
                                 w2m[psl][:],
                                 v_[:, h * 128:(h + 1) * 128], start=True,
                                 stop=False, skip_group_check=True)
                        ii = t.matmul(ps_ov[:, psl * 512:psl * 512 + 128],
                                      ut_bf[(h % 2) * 64:(h % 2) * 64 + 64,
                                            (h // 2) * 128:(h // 2 + 1) * 128],
                                      fslice(SVb_c, h, 128), start=False,
                                      stop=True, skip_group_check=True)
                        S.te.inc(ii, 1)

                @block.vector
                def _(v):
                    for h in range(H):
                        psl = h % 2
                        S.te.wait_at(v, te_ov_base + h + 1)
                        ii = v.tensor_copy(ov_sb[:, h * 128:(h + 1) * 128],
                                           ps_ov[:, psl * 512:psl * 512 + 128])
                        S.ove.inc(ii, 1)

                p3_sc_jobs.append(("squares", S.ove.n, ch))

                @block.vector
                def _(v, ch=ch):
                    S.sqe.wait_at(v, sqe_b3 + (ch + 1) * H)
                    v.tensor_scalar(qk_a[:, 0:64], ssq[:], 1.0 / DV, EPS,
                                    ALU.mult, ALU.add)
                    rsqrt_dve(v, rstd_o[:], qk_a[:, 0:64], qk_b[:, 0:64], 64)
                    S.tpe.wait_at(v, marks["og"])
                    for h in range(H):
                        v.scalar_tensor_tensor(
                            t1g[:], ov_sb[:, h * 128:(h + 1) * 128],
                            rstd_o[:, h:h + 1], gnw_bc[:], ALU.mult, ALU.mult)
                        S.gate.inc(v.tensor_tensor(
                            og_bf[:, h * 128:(h + 1) * 128], t1g[:],
                            sg[:, h * 128:(h + 1) * 128], ALU.mult), 1)

                te_og_base = S.te.n

                @block.tensor
                def _(t):
                    S.gate.wait(t)
                    for kc in range(DCH):
                        tp_rot.acquire(t)
                        tsl = tp_rot.emitted()
                        ii = t.transpose(ps_tp[:, tsl * 1024:tsl * 1024 + 128],
                                         og_bf[:, kc * 128:(kc + 1) * 128],
                                         ident[:])
                        S.te.inc(ii, 1)

                @block.vector
                def _(v, ch=ch, g=g):
                    cq = ch % 4
                    S.woev.wait_at(v, g * DCH)
                    for kc in range(DCH):
                        tsl = (tp_rot.idx - DCH + kc) % 2
                        S.te.wait_at(v, te_og_base + kc + 1)
                        ii = v.tensor_copy(
                            ogT_g[:, kc * 512 + cq * 128:
                                  kc * 512 + (cq + 1) * 128],
                            ps_tp[:, tsl * 1024:tsl * 1024 + 128])
                        if kc < DCH - 1:
                            S.tpe.inc(ii, 1)
                    S.oge.inc(ii, 1)
                    S.tpe.inc(v.engine_nop(), 1)
                    marks["og"] = S.tpe.n

            def group_loads(g):
                @block.sync
                def _(sy, g=g):
                    S.mrg.wait_at(sy, g * DCH)
                    S.st.wait(sy)
                    for half in range(2):
                        ii = sy.dma_start(
                            out=hr_g.ap().rearrange("p (f t) -> p f t", t=512)
                            [:, half * (NFT // 2):(half + 1) * (NFT // 2), :],
                            in_=d_hr[half * (NFT // 2):(half + 1) * (NFT // 2),
                                     :, g * 512:(g + 1) * 512]
                            .rearrange("f p t -> p f t"))
                        S.hrl.inc(ii, 16)
                    ii = sy.dma_start(
                        out=xat.ap().rearrange("p (c t) -> p c t", t=512),
                        in_=xownT[:, g * 512:(g + 1) * 512]
                        .rearrange("(c p) t -> p c t", p=128))
                    S.xtl.inc(ii, 16)

                @block.gpsimd
                def _(gp, g=g):
                    for dsl in range(DCH):
                        wb = dsl % 2
                        S.woev.wait_at(gp, g * DCH + dsl - 1)
                        S.wol.inc(gp.dma_start(out=wobuf[wb][:],
                                               in_=d_wo[dsl]), 16)
                    for dsl in range(DCH):
                        wb = dsl % 2
                        S.mrg.wait_at(gp, g * DCH + dsl - 1)
                        S.wpl.inc(gp.dma_start(out=wptb[wb][:],
                                               in_=d_wpt[dsl]), 16)

            def emit_proj_piece(g, dsl):
                @block.tensor
                def _(t, g=g, dsl=dsl):
                    wb = dsl % 2
                    if dsl == 0:
                        S.hrl.wait_at(t, (g + 1) * 32)
                    S.wpl.wait_at(t, (g * DCH + dsl + 1) * 16)
                    S.mrg.wait_at(t, g * DCH + dsl - 1)
                    S.woev.wait(t)
                    S.okf.wait(t)
                    psl = dsl % 2
                    for fc2 in range(NFT):
                        ii = t.matmul(ps_ok[:, psl * 512:(psl + 1) * 512],
                                      wptb[wb][:, fc2 * 128:(fc2 + 1) * 128],
                                      hr_g[:, fc2 * 512:(fc2 + 1) * 512],
                                      start=(fc2 == 0),
                                      stop=(fc2 == NFT - 1))
                    S.te.inc(ii, 1)

                @block.vector
                def _(v, g=g, dsl=dsl):
                    S.te.wait_at(v, S.te.n)
                    S.xtl.wait_at(v, (g + 1) * 16)
                    psl = dsl % 2
                    ii = v.scalar_tensor_tensor(
                        xat[:, dsl * 512:(dsl + 1) * 512],
                        ps_ok[:, psl * 512:(psl + 1) * 512],
                        amc_col[:, 0:1],
                        xat[:, dsl * 512:(dsl + 1) * 512],
                        ALU.mult, ALU.add)
                    S.mrg.inc(ii, 1)
                    S.okf.inc(v.engine_nop(), 1)

            def emit_group_epilogue(g):
                @block.tensor
                def _(t, g=g):
                    S.oge.wait(t)
                    S.okf.wait(t)
                    for dsl in range(DCH):
                        wb = dsl % 2
                        S.wol.wait_at(t, (g * DCH + dsl + 1) * 16)
                        S.woev.wait_at(t, g * DCH + dsl - 1)
                        psl = dsl % 2
                        for kc in range(DCH):
                            ii = t.matmul(ps_ok[:, psl * 512:(psl + 1) * 512],
                                          wobuf[wb][:, kc * 128:(kc + 1) * 128],
                                          ogT_g[:, kc * 512:(kc + 1) * 512],
                                          start=(kc == 0), stop=(kc == DCH - 1))
                        S.te.inc(ii, 1)

                @block.vector
                def _(v, g=g):
                    base = S.te.n - DCH
                    S.xtl.wait(v)
                    for dsl in range(DCH):
                        psl = dsl % 2
                        S.te.wait_at(v, base + dsl + 1)
                        ii = v.tensor_tensor(xat[:, dsl * 512:(dsl + 1) * 512],
                                             xat[:, dsl * 512:(dsl + 1) * 512],
                                             ps_ok[:, psl * 512:(psl + 1) * 512],
                                             ALU.add)
                        S.woev.inc(ii, 1)
                    S.okf.inc(v.engine_nop(), 1)

            def emit_store(g):
                @block.sync
                def _(sy, g=g):
                    S.mrg.wait(sy)
                    S.woev.wait(sy)
                    ii = sy.dma_start(
                        out=outT_h[:, g * 512:(g + 1) * 512]
                        .rearrange("(c p) t -> p c t", p=128),
                        in_=xat.ap().rearrange("p (c t) -> p c t", t=512))
                    S.st.inc(ii, 16)

            @block.vector
            def _(v):
                v.tensor_copy(skb3[0][:], SKs[:])
                ii = v.tensor_copy(svb3[0][:], SVs[:])
                S.stc.inc(ii, 1)
                S.stu.inc(v.engine_nop(), 1)
                S.okf.inc(v.engine_nop(), 1)

            INTERLEAVE_PJ = False
            se_b3 = S.se.n
            sqe_b3 = S.sqe.n
            p3_sc_jobs = []
            group_loads(0)
            for ch in range(NOWN):
                ring_load(ch)
                phase1(ch)
                if ch > 0:
                    phase2(ch - 1, (ch - 1) // 4)
                    if INTERLEAVE_PJ:
                        g_p, i0 = divmod(ch - 1, 4)
                        for dsl in range(i0 * 4, i0 * 4 + 4):
                            emit_proj_piece(g_p, dsl)
                    if (ch - 1) % 4 == 3:
                        g = (ch - 1) // 4
                        emit_group_epilogue(g)
                        if not INTERLEAVE_PJ:
                            for dsl in range(DCH):
                                emit_proj_piece(g, dsl)
                        emit_store(g)
                        if g + 1 < NGRP:
                            group_loads(g + 1)
            phase2(NOWN - 1, (NOWN - 1) // 4)
            if INTERLEAVE_PJ:
                g_p, i0 = divmod(NOWN - 1, 4)
                for dsl in range(i0 * 4, i0 * 4 + 4):
                    emit_proj_piece(g_p, dsl)
                emit_group_epilogue(NGRP - 1)
            else:
                emit_group_epilogue(NGRP - 1)
                for dsl in range(DCH):
                    emit_proj_piece(NGRP - 1, dsl)
            emit_store(NGRP - 1)

            @block.scalar
            def _(s):
                for kind, m, ch in p3_sc_jobs:
                    ok_ = okr[ch % 2]
                    if kind == "exps":
                        S.ve.wait_at(s, m)
                        for h in range(H):
                            ii = s.activation(ok_[:, h * 64:(h + 1) * 64],
                                              ok_[:, h * 64:(h + 1) * 64],
                                              AF.Exp, scale=SCALE,
                                              bias=nmx16[:, h:h + 1],
                                              accum_out=sume[:, h:h + 1])
                        S.se.inc(ii, 1)
                    else:
                        S.ove.wait_at(s, m)
                        for h in range(H):
                            ii = s.activation(ok_[:, 0:128],
                                              ov_sb[:, h * 128:(h + 1) * 128],
                                              AF.Square,
                                              accum_out=ssq[:, h:h + 1])
                            S.sqe.inc(ii, 1)

            @block.sync
            def _(sy):
                S.st.wait(sy)
                S.sp.wait(sy)
    return nc



def _stage_inputs(x, w, cfg):
    T, TC, TP, NPRE = cfg.T, cfg.TC, cfg.TP, cfg.NPRE
    xT = np.ascontiguousarray(x.T)
    shared = dict(
        wq=np.ascontiguousarray(w["Wq"]), wk=np.ascontiguousarray(w["Wk"]),
        wv=np.ascontiguousarray(w["Wv"]), wg=np.ascontiguousarray(w["Wg"]),
        ws=np.ascontiguousarray(w["Ws"]), wo=np.ascontiguousarray(w["Wo"]),
        wfcT=np.ascontiguousarray(w["w_fc"].T),
        wprojT=np.ascontiguousarray(w["w_proj"].T),
        gnw=np.ascontiguousarray(w["g_norm_weight"].reshape(1, DV)),
    )
    maps = []
    for c in range(NCORE):
        t0 = c * TC
        xpre_tok = np.zeros((TP, D), np.float32)
        xpre_tok[:t0] = x[:t0]
        xpreT = np.zeros((D, TP), np.float32)
        xpreT[:, :t0] = xT[:, :t0]
        vb = np.full((128, NPRE), -100.0, np.float32)
        for ch in range(NPRE):
            n = min(max(t0 - ch * C, 0), C)
            if n > 0:
                vb[:n, ch] = 0.0
        maps.append(dict(
            xpre_tok=xpre_tok, xpreT=xpreT,
            xown_tok=np.ascontiguousarray(x[t0:t0 + TC]),
            xownT=np.ascontiguousarray(xT[:, t0:t0 + TC]),
            vbias=vb, **shared))
    return maps


def kernel(L0_attn_hidden_states, L0_attn_Wq, L0_attn_Wk, L0_attn_Wv,
           L0_attn_Wg, L0_attn_Ws, L0_attn_Wo, L0_attn_g_norm_weight,
           L0_ffn_w_fc, L0_ffn_w_proj):
    x = np.asarray(L0_attn_hidden_states, np.float32)
    B, T, _ = x.shape
    cfg = Cfg(T)
    w = dict(Wq=L0_attn_Wq, Wk=L0_attn_Wk, Wv=L0_attn_Wv, Wg=L0_attn_Wg,
             Ws=L0_attn_Ws, Wo=L0_attn_Wo,
             g_norm_weight=L0_attn_g_norm_weight,
             w_fc=L0_ffn_w_fc, w_proj=L0_ffn_w_proj)
    w = {k: np.asarray(v, np.float32) for k, v in w.items()}
    maps = _stage_inputs(x[0], w, cfg)
    nc = build_program(cfg, debug=False)
    res = run_bass_kernel_spmd(nc, maps, core_ids=list(range(NCORE)))
    outT = np.concatenate([np.asarray(r["outT"]) for r in res.results], axis=1)
    return np.ascontiguousarray(outT.T).reshape(1, T, D).astype(np.float32)


# version 09 (final)
# speedup vs baseline: 1.0118x; 1.0118x over previous
import contextlib
import numpy as np
import ml_dtypes
import concourse.bass as bass
import concourse.mybir as mybir
from concourse.bass_utils import run_bass_kernel_spmd

F32 = mybir.dt.float32
BF16 = mybir.dt.bfloat16
I32 = mybir.dt.int32
MAGIC_F = float(np.int32(0x5f3759df).view(np.float32))
AF = mybir.ActivationFunctionType
ALU = mybir.AluOpType
AX = mybir.AxisListType

D = 2048
H = 16
DK = 64
DV = 128
M = 64
KD = H * DK
VD = H * DV
MD = H * M
FF = 5120
C = 128
EPS = 1e-5
CLAMP = 32.0
SCALE = DK ** -0.5
NCORE = 8
DCH = D // 128
NFT = FF // 128


class Cfg:
    def __init__(self, T):
        assert T % (NCORE * 512) == 0
        self.T = T
        self.TC = T // NCORE
        self.TP = self.TC * (NCORE - 1)
        self.NPRE = self.TP // C
        self.NOWN = self.TC // C
        self.NGRP = self.TC // 512
        self.PREG = 4
        assert self.NPRE % self.PREG == 0


class _SemRef:
    __slots__ = ("cell",)

    def __init__(self, cell):
        self.cell = cell

    @property
    def h(self):
        return self.cell[0]

    @property
    def n(self):
        return self.cell[1]

    def inc(self, instr, k=1):
        instr.then_inc(self.cell[0], k)
        self.cell[1] += k
        return instr

    def wait(self, eng, lag=0):
        if self.cell[1] - lag > 0:
            eng.wait_ge(self.cell[0], self.cell[1] - lag)

    def wait_at(self, eng, val):
        if val > 0:
            eng.wait_ge(self.cell[0], val)


class Sems:
    def __init__(self, nc, stack):
        object.__setattr__(self, "_nc", nc)
        object.__setattr__(self, "_stack", stack)
        object.__setattr__(self, "_tab", {})

    def __getattr__(self, name):
        tab = object.__getattribute__(self, "_tab")
        if name not in tab:
            nc = object.__getattribute__(self, "_nc")
            stack = object.__getattribute__(self, "_stack")
            h = stack.enter_context(nc.semaphore(name))
            tab[name] = [h, 0]
        return _SemRef(tab[name])


class SlotPool:

    def __init__(self, n, sem):
        self.n = n
        self.sem = sem
        self.marks = [0] * n
        self.idx = 0

    def acquire(self, eng):
        i = self.idx % self.n
        self.sem.wait_at(eng, self.marks[i])
        self.idx += 1
        return i

    def release(self, i):
        self.marks[i % self.n] = self.sem.n


class Rot:

    def __init__(self, n, sem):
        self.n = n
        self.sem = sem
        self.idx = 0
        self.base = sem.n

    def acquire(self, eng):
        self.sem.wait_at(eng, self.base + self.idx - self.n + 1)

    def emitted(self):
        i = self.idx % self.n
        self.idx += 1
        return i


def build_program(cfg, debug=False):
    nc = bass.Bass("TRN2", target_bir_lowering=False, debug=False)
    TC, TP, NPRE, NOWN, NGRP, PREG = (
        cfg.TC, cfg.TP, cfg.NPRE, cfg.NOWN, cfg.NGRP, cfg.PREG)
    NT4 = TC // 512

    ap = lambda t: t.ap()
    I = lambda n, s, d=F32: ap(nc.dram_tensor(n, s, d, kind="ExternalInput"))
    xpre_tok = I("xpre_tok", [TP, D])
    xpreT = I("xpreT", [D, TP])
    xown_tok = I("xown_tok", [TC, D])
    xownT = I("xownT", [D, TC])
    vbias = I("vbias", [128, NPRE])
    wq_h = I("wq", [D, KD])
    wk_h = I("wk", [D, KD])
    wv_h = I("wv", [D, VD])
    wg_h = I("wg", [D, VD])
    ws_h = I("ws", [D, MD])
    wo_h = I("wo", [VD, D])
    wfcT_h = I("wfcT", [D, FF])
    wprojT_h = I("wprojT", [FF, D])
    gnw_h = I("gnw", [1, DV])

    outT_h = ap(nc.dram_tensor("outT", [D, TC], F32, kind="ExternalOutput"))

    dk = "ExternalOutput" if debug else "Internal"
    d_q = ap(nc.dram_tensor("d_q", [NOWN, 128, KD], BF16, kind=dk))
    d_kT = ap(nc.dram_tensor("d_kT", [NOWN, 128, KD], BF16, kind=dk))
    d_kt = ap(nc.dram_tensor("d_kt", [NOWN, 128, KD], BF16, kind=dk))
    d_ef = ap(nc.dram_tensor("d_ef", [NOWN, 128, MD], BF16, kind=dk))
    d_et = ap(nc.dram_tensor("d_et", [NOWN, 128, MD], BF16, kind=dk))
    d_v = ap(nc.dram_tensor("d_v", [NOWN, 128, VD], BF16, kind=dk))
    d_sg = ap(nc.dram_tensor("d_sg", [NOWN, 128, VD], BF16, kind=dk))
    d_hr = ap(nc.dram_tensor("d_hr", [NFT, 128, TC], BF16, kind=dk))
    d_wpt = ap(nc.dram_tensor("d_wpt", [DCH, 128, NFT * 128], BF16, kind=dk))
    d_wo = ap(nc.dram_tensor("d_wo", [DCH, 128, DCH * 128], BF16, kind=dk))
    if debug:
        d_X = ap(nc.dram_tensor("d_X", [128, DCH * MD], F32, kind=dk))
        d_A = ap(nc.dram_tensor("d_A", [1, MD], F32, kind=dk))
        d_SK = ap(nc.dram_tensor("d_SK", [128, 8 * M], F32, kind=dk))
        d_SV = ap(nc.dram_tensor("d_SV", [128, 8 * DV], F32, kind=dk))
        d_xnT = ap(nc.dram_tensor("d_xnT", [128, DCH * TC], BF16, kind=dk))
        d_xat1 = ap(nc.dram_tensor("d_xat1", [NGRP, 128, DCH * 512], F32, kind=dk))
        d_ogT = ap(nc.dram_tensor("d_ogT", [NGRP, 128, DCH * 512], BF16, kind=dk))

    st = contextlib.ExitStack()
    with st:
        sb = lambda n, s, d: st.enter_context(nc.sbuf_tensor(n, s, d))
        S = Sems(nc, st)
        block = st.enter_context(nc.Block())

        mask01 = sb("mask01", [128, 128], BF16)
        ident = sb("ident", [128, 128], BF16)
        iota_f = sb("iota_f", [128, 128], F32)
        ones_bf = sb("ones_bf", [128, 128], BF16)
        onesc_f32 = sb("onesc_f32", [128, 1], F32)
        ones1_f32 = sb("ones1_f32", [1, 128], F32)
        eps_col = sb("eps_col", [128, 1], F32)
        zero_col = sb("zero_col", [128, 1], F32)
        gnw_bc = sb("gnw_bc", [128, DV], F32)
        vb_sb = sb("vb_sb", [128, NPRE], F32)
        thr_col = sb("thr_col", [128, 2], F32)
        nthr_col = sb("nthr_col", [128, 2], F32)
        amc_col = sb("amc_col", [128, 1], F32)
        srow = sb("srow", [1, 16], F32)
        SKs = sb("SKs", [128, 8 * M], F32)
        SVs = sb("SVs", [128, 8 * DV], F32)
        A_row = sb("A_row", [1, MD], F32)
        bnst = sb("bnst", [128, 24], F32)
        mv = sb("mv", [128, 2], F32)
        rstd_c = sb("rstd_c", [128, 1], F32)
        nmean_c = sb("nmean_c", [128, 1], F32)
        nrmu_c = sb("nrmu_c", [128, 1], F32)
        mgc = sb("mgc", [128, 1], F32)
        qk_a = sb("qk_a", [128, 128], F32)
        qk_b = sb("qk_b", [128, 128], F32)

        ACT_SET_ON = False

        def act_set(s, idx=6):
            if not ACT_SET_ON:
                return
            s.add_instruction(mybir.InstLoadActFuncSet(
                name=nc.get_next_instruction_name(),
                act_func_set_id=idx, ins=[], outs=[]))

        def barrier():
            tgt = S.bar.n + 5
            for sec in (block.sync, block.gpsimd, block.scalar, block.vector,
                        block.tensor):
                def _f(e, tgt=tgt):
                    S.bar.inc(e.drain(), 1)
                    e.wait_ge(S.bar.h, tgt)
                sec(_f)

        def rsqrt_dve(v, out, q1, q2, n):
            v.tensor_scalar(q2.bitcast(I32), q1.bitcast(I32), 1, None,
                            ALU.arith_shift_right)
            v.tensor_tensor(q2.bitcast(I32),
                            mgc.ap().bitcast(I32).broadcast_to([128, n]),
                            q2.bitcast(I32), ALU.subtract)
            for it in range(3):
                v.tensor_tensor(out, q1, q2, ALU.mult)
                v.tensor_tensor(out, out, q2, ALU.mult)
                v.tensor_scalar(out, out, -0.5, 1.5, ALU.mult, ALU.add)
                if it < 2:
                    v.tensor_tensor(q2, q2, out, ALU.mult)
                else:
                    S.ve.inc(v.tensor_tensor(out, q2, out, ALU.mult), 1)

        def bcast_row(row_ap, out_sb_ap, ps_ap, extra=()):
            @block.tensor
            def _(t):
                S.ve.wait(t)
                for s_ in extra:
                    s_.wait(t)
                S.te.inc(t.matmul(ps_ap, ones1_f32[:], row_ap,
                                  start=True, stop=True), 1)

            @block.vector
            def _(v):
                S.te.wait(v)
                S.ve.inc(v.tensor_copy(out_sb_ap, ps_ap), 1)

        @block.gpsimd
        def _(g):
            g.iota(iota_f[:], [[1, 128]], base=0, channel_multiplier=-1,
                   allow_small_or_imprecise_dtypes=True)
            g.memset(ones_bf[:], 1.0)
            g.memset(ones1_f32[:], 1.0)
            g.memset(onesc_f32[:], 1.0)
            g.memset(eps_col[:], EPS)
            g.memset(zero_col[:], 0.0)
            g.memset(mgc[:], MAGIC_F)
            S.gp.inc(g.affine_select(ident[:], ones_bf[:], [[-1, 128]],
                                     ALU.is_equal, 0.0, base=0,
                                     channel_multiplier=1), 1)

        @block.sync
        def _(sy):
            S.ld.inc(sy.dma_start(out=vb_sb[:], in_=vbias[:]), 16)

        @block.scalar
        def _(s):
            act_set(s, 6)

        @block.vector
        def _(v):
            S.gp.wait(v)
            S.ve.inc(v.tensor_scalar(mask01[:], iota_f[:], 0.0, None, ALU.is_ge), 1)
            v.memset(SKs[:], 0.0)
            v.memset(SVs[:], 0.0)
            S.ve.inc(v.memset(A_row[:], 0.0), 1)

        with nc.sbuf_tensor("gnw_sb", [1, DV], F32) as gnw_sb, \
                nc.psum_tensor("ps_g0", [128, 512], F32) as ps_g0:
            @block.sync
            def _(sy):
                S.ld.inc(sy.dma_start(out=gnw_sb[:], in_=gnw_h[:]), 16)

            @block.tensor
            def _(t):
                S.ld.wait(t)
                S.gp.wait(t)
                S.te.inc(t.matmul(ps_g0[:, 0:DV], ones1_f32[:], gnw_sb[:],
                                  start=True, stop=True), 1)

            @block.vector
            def _(v):
                S.te.wait(v)
                S.ve.inc(v.tensor_scalar(gnw_bc[:], ps_g0[:, 0:DV], 0.5,
                                         None, ALU.mult), 1)

        barrier()

        p12 = st.enter_context(contextlib.ExitStack())
        X_bf = p12.enter_context(nc.sbuf_tensor("X_bf", [128, DCH * MD], BF16))
        rstd_w = p12.enter_context(nc.sbuf_tensor("rstd_w", [128, 128], F32))
        rmu_w = p12.enter_context(nc.sbuf_tensor("rmu_w", [128, 128], F32))

        with contextlib.ExitStack() as p1:
            sb1 = lambda n, s, d: p1.enter_context(nc.sbuf_tensor(n, s, d))
            ps1 = lambda n, s, d: p1.enter_context(nc.psum_tensor(n, s, d))
            ws_bf = sb1("ws_bf", [128, DCH * MD], BF16)
            csw_row = sb1("csw_row", [1, MD], F32)
            csw_bc = sb1("csw_bc", [128, MD], F32)
            X_sb = sb1("X_sb", [128, DCH * MD], F32)
            crow_sb = csw_row
            cbc_sb = csw_bc
            xtok2 = [sb1(f"xtok{i}", [128, D], F32) for i in range(2)]
            xTb2 = [sb1(f"xTb{i}", [128, D], BF16) for i in range(2)]
            xp_g = sb1("xp_g", [128, PREG * D], BF16)
            e_g = sb1("e_g", [128, PREG * MD], BF16)
            onemu_g = sb1("onemu_g", [128, 2 * PREG], BF16)
            s_st1 = sb1("s_st1", [128, MD], F32)
            s_bfc = [sb1(f"s_bfc{i}", [128, MD], BF16)
                     for i in range(2)]

            ps_s = ps1("ps_s", [128, MD], F32)
            ps_x = ps1("ps_x", [128, MD], F32)
            ps_x2 = ps1("ps_x2", [128, MD], F32)
            ps_a1 = ps1("ps_a1", [128, 512], F32)
            ps_a2 = ps1("ps_a2", [128, 512], F32)

            @block.gpsimd
            def _(g):
                S.ca.inc(g.dma_start(
                    out=ws_bf.ap().rearrange("p (c f) -> p c f", f=MD),
                    in_=ws_h.rearrange("(c p) f -> p c f", p=128)), 16)
                for ci in range(PREG):
                    g.memset(onemu_g[:, 2 * ci:2 * ci + 1], 1.0)
                S.gp.inc(g.engine_nop(), 1)

            for half in range(2):
                @block.tensor
                def _(t, half=half):
                    S.ca.wait(t)
                    if half == 1:
                        S.ve.wait(t)
                    for dc in range(DCH):
                        ii = t.matmul(
                            ps_a1[0:1, 0:512], ones_bf[:, 0:1],
                            ws_bf[:, dc * MD + half * 512:dc * MD + half * 512 + 512],
                            start=(dc == 0), stop=(dc == DCH - 1))
                    S.te.inc(ii, 1)

                @block.vector
                def _(v, half=half):
                    S.te.wait(v)
                    S.ve.inc(v.tensor_copy(csw_row[0:1, half * 512:half * 512 + 512],
                                           ps_a1[0:1, 0:512]), 1)

            bcast_row(csw_row[0:1, 0:512], csw_bc[:, 0:512], ps_s[:, 0:512])
            bcast_row(csw_row[0:1, 512:MD], csw_bc[:, 512:MD], ps_s[:, 0:512])

            @block.vector
            def _(v):
                v.memset(X_sb[:], 0.0)
                S.ve.inc(v.engine_nop(), 1)

            barrier()

            ngrp_pre = NPRE // PREG
            p1_exp_jobs = []
            onemu_marks = {}
            sp_marks = {}

            def emit_amm(ch):
                @block.tensor
                def _(t, ch=ch):
                    ci = ch % PREG
                    S.sev.wait_at(t, ch + 1)
                    S.ve.wait_at(t, onemu_marks[ch])
                    t.matmul(ps_a1[0:2, 0:512],
                             onemu_g[:, 2 * ci:2 * ci + 2],
                             e_g[:, ci * MD:ci * MD + 512],
                             start=(ch == 0), stop=(ch == NPRE - 1),
                             skip_group_check=True)
                    ii = t.matmul(ps_a2[0:2, 0:512],
                                  onemu_g[:, 2 * ci:2 * ci + 2],
                                  e_g[:, ci * MD + 512:(ci + 1) * MD],
                                  start=(ch == 0), stop=(ch == NPRE - 1),
                                  skip_group_check=True)
                    S.te.inc(ii, 1)

            def emit_xaccum(gi):
                @block.tensor
                def _(t, gi=gi):
                    S.sev.wait_at(t, (gi + 1) * PREG)
                    S.ve.wait(t)
                    for dc in range(DCH):
                        pp = ps_x if dc % 2 == 0 else ps_x2
                        S.xev.wait_at(t, gi * DCH + dc - 1)
                        for c2 in range(PREG):
                            for half in range(2):
                                ii = t.matmul(
                                    pp[:, half * 512:(half + 1) * 512],
                                    xp_g[:, c2 * D + dc * C:c2 * D + (dc + 1) * C],
                                    e_g[:, c2 * MD + half * 512:
                                        c2 * MD + (half + 1) * 512],
                                    start=(c2 == 0), stop=(c2 == PREG - 1),
                                    skip_group_check=True)
                        S.te.inc(ii, 1)

                @block.vector
                def _(v, gi=gi):
                    base = S.te.n - DCH
                    for dc in range(DCH):
                        pp = ps_x if dc % 2 == 0 else ps_x2
                        S.te.wait_at(v, base + dc + 1)
                        ii = v.tensor_tensor(X_sb[:, dc * MD:(dc + 1) * MD],
                                             X_sb[:, dc * MD:(dc + 1) * MD],
                                             pp[:], ALU.add)
                        S.xev.inc(ii, 1)

            for ch in range(NPRE):
                gi, ci = divmod(ch, PREG)
                buf = ch % 2

                @block.sync
                def _(sy, ch=ch, buf=buf):
                    S.pxf.wait_at(sy, ch - 1)
                    S.ld.inc(sy.dma_start(
                        out=xtok2[buf][:],
                        in_=xpre_tok[ch * C:(ch + 1) * C, :]), 16)

                @block.gpsimd
                def _(g, ch=ch, buf=buf):
                    S.sev.wait_at(g, ch - 1)
                    S.ca.inc(g.dma_start(
                        out=xTb2[buf].ap().rearrange("p (c t) -> p c t", t=C),
                        in_=xpreT[:, ch * C:(ch + 1) * C]
                        .rearrange("(c p) t -> p c t", p=128)), 16)

                @block.tensor
                def _(t, ch=ch, buf=buf, ci=ci, gi=gi):
                    S.ca.wait(t)
                    S.sev.wait_at(t, ch - 1)
                    if ci == 1 and gi > 0:
                        S.xev.wait_at(t, gi * DCH)
                    pp = ps_s if ch % 2 == 0 else ps_x2
                    for half in range(2):
                        for dc in range(DCH):
                            ii = t.matmul(
                                pp[:, half * 512:(half + 1) * 512],
                                xTb2[buf][:, dc * C:(dc + 1) * C],
                                ws_bf[:, dc * MD + half * 512:
                                      dc * MD + (half + 1) * 512],
                                start=(dc == 0), stop=(dc == DCH - 1))
                    S.te.inc(ii, 1)

                sp_marks[ch] = S.te.n
                if ch > 0:
                    emit_amm(ch - 1)
                if ci == 0 and gi > 0:
                    emit_xaccum(gi - 1)

                @block.vector
                def _(v, ch=ch, buf=buf):
                    S.ld.wait(v)
                    for j in range(4):
                        ii = v.bn_stats(bnst[:, j * 6:(j + 1) * 6],
                                        xtok2[buf][:, j * 512:(j + 1) * 512])
                    S.ve.inc(ii, 1)
                    S.ve.wait(v)
                    S.ve.inc(v.bn_aggr(mv[:], bnst[:]), 1)

                @block.vector
                def _(v, ch=ch, buf=buf, ci=ci):
                    v.tensor_copy(qk_b[:], gnw_bc[:, 0:128])
                    v.tensor_scalar(qk_a[:],
                                    mv[:, 1:2].broadcast_to([128, 128]),
                                    EPS, None, ALU.add)
                    rsqrt_dve(v, rstd_w[:], qk_a[:], qk_b[:], 128)
                    v.tensor_tensor(rmu_w[:],
                                    mv[:, 0:1].broadcast_to([128, 128]),
                                    rstd_w[:], ALU.mult)
                    S.ve.inc(v.tensor_copy(onemu_g[:, 2 * ci + 1:2 * ci + 2],
                                           rmu_w[:, 0:1]), 1)
                    S.ve.inc(v.tensor_scalar(rmu_w[:], rmu_w[:], -1.0, None,
                                             ALU.mult), 1)
                    S.ve.inc(v.tensor_scalar(xp_g[:, ci * D:(ci + 1) * D],
                                             xtok2[buf][:], rstd_w[:, 0:1],
                                             None, ALU.mult), 1)
                    S.pxf.inc(v.engine_nop(), 1)

                onemu_marks[ch] = S.ve.n

                @block.vector
                def _(v, ch=ch, ci=ci):
                    S.te.wait_at(v, sp_marks[ch])
                    pp = ps_s if ch % 2 == 0 else ps_x2
                    S.ve.inc(v.tensor_scalar(s_st1[:], pp[:], rstd_w[:, 0:1],
                                             None, ALU.mult), 1)
                    S.ve.wait(v)
                    S.ve.inc(v.scalar_tensor_tensor(s_st1[:], csw_bc[:],
                                                    rmu_w[:, 0:1], s_st1[:],
                                                    ALU.mult, ALU.add), 1)
                    S.ve.wait(v)
                    S.ve.inc(v.tensor_scalar(s_bfc[ch % 2][:], s_st1[:],
                                             -CLAMP, CLAMP,
                                             ALU.max, ALU.min), 1)

                te_gate = S.te.n if (ci == 0 and gi > 0) else None
                p1_exp_jobs.append((S.ve.n, ch, ci, te_gate))

            emit_amm(NPRE - 1)
            emit_xaccum(ngrp_pre - 1)

            @block.scalar
            def _(s):
                for m, ch, ci, te_gate in p1_exp_jobs:
                    S.ve.wait_at(s, m)
                    if te_gate is not None:
                        S.te.wait_at(s, te_gate)
                    S.sev.inc(s.activation(e_g[:, ci * MD:(ci + 1) * MD],
                                           s_bfc[ch % 2][:], AF.Exp,
                                           bias=vb_sb[:, ch:ch + 1]), 1)

            @block.vector
            def _(v):
                S.te.wait(v)
                v.tensor_copy(A_row[0:1, 0:512], ps_a1[0:1, 0:512])
                S.ve.inc(v.tensor_copy(A_row[0:1, 512:MD], ps_a2[0:1, 0:512]), 1)
                v.tensor_copy(s_st1[0:2, 0:512], ps_a1[0:2, 0:512])
                S.ve.inc(v.tensor_copy(s_st1[0:2, 512:MD], ps_a2[0:2, 0:512]), 1)

            @block.gpsimd
            def _(g):
                S.ve.wait(g)
                S.ca.inc(g.dma_start(out=crow_sb[0:1, 0:512],
                                     in_=s_st1[1:2, 0:512]), 16)
                S.ca.inc(g.dma_start(out=crow_sb[0:1, 512:MD],
                                     in_=s_st1[1:2, 512:MD]), 16)

            bcast_row(crow_sb[0:1, 0:512], cbc_sb[:, 0:512], ps_s[:, 0:512],
                      extra=(S.ca,))
            bcast_row(crow_sb[0:1, 512:MD], cbc_sb[:, 512:MD], ps_s[:, 0:512])

            @block.vector
            def _(v):
                S.ve.wait(v)
                S.xev.wait(v)
                for dc in range(DCH):
                    v.tensor_tensor(X_sb[:, dc * MD:(dc + 1) * MD],
                                    X_sb[:, dc * MD:(dc + 1) * MD],
                                    cbc_sb[:], ALU.subtract)
                S.ve.inc(v.engine_nop(), 1)
                S.ve.wait(v)
                for dc in range(DCH):
                    ii = v.tensor_copy(X_bf[:, dc * MD:(dc + 1) * MD],
                                       X_sb[:, dc * MD:(dc + 1) * MD])
                S.ve.inc(ii, 1)

            if debug:
                @block.sync
                def _(sy):
                    S.ve.wait(sy)
                    S.st.inc(sy.dma_start(out=d_X, in_=X_sb[:]), 16)
                    S.st.inc(sy.dma_start(out=d_A, in_=A_row[:]), 16)

            barrier()

        xnT = p12.enter_context(nc.sbuf_tensor("xnT", [128, DCH * TC], BF16))
        with contextlib.ExitStack() as p2a:
            sb2 = lambda n, s, d: p2a.enter_context(nc.sbuf_tensor(n, s, d))
            xo2 = [sb2(f"xo{i}", [128, D], F32) for i in range(2)]
            xn2 = [sb2(f"xn{i}", [128, D], BF16) for i in range(2)]
            ps_tp = p2a.enter_context(nc.psum_tensor("ps_tp_a", [128, 2048], BF16))
            tp_rot = Rot(2, S.tpe)

            for ch in range(NOWN):
                buf = ch % 2

                @block.sync
                def _(sy, ch=ch, buf=buf):
                    S.oxf.wait_at(sy, ch - 1)
                    S.ld.inc(sy.dma_start(out=xo2[buf][:],
                                          in_=xown_tok[ch * C:(ch + 1) * C, :]), 16)

                @block.vector
                def _(v, buf=buf):
                    S.ld.wait(v)
                    for j in range(4):
                        ii = v.bn_stats(bnst[:, j * 6:(j + 1) * 6],
                                        xo2[buf][:, j * 512:(j + 1) * 512])
                    S.ve.inc(ii, 1)
                    S.ve.wait(v)
                    S.ve.inc(v.bn_aggr(mv[:], bnst[:]), 1)

                @block.vector
                def _(v, ch=ch, buf=buf):
                    v.tensor_copy(qk_b[:], gnw_bc[:, 0:128])
                    v.tensor_scalar(qk_a[:],
                                    mv[:, 1:2].broadcast_to([128, 128]),
                                    EPS, None, ALU.add)
                    rsqrt_dve(v, rstd_w[:], qk_a[:], qk_b[:], 128)
                    S.ve.inc(v.tensor_scalar(rmu_w[:],
                                             mv[:, 0:1].broadcast_to([128, 128]),
                                             -1.0, None, ALU.mult), 1)
                    S.tpe.wait_at(v, tp_rot.base + (ch - 1) * DCH)
                    S.ve.inc(v.tensor_scalar(xn2[buf][:], xo2[buf][:],
                                             rmu_w[:, 0:1], rstd_w[:, 0:1],
                                             ALU.add, ALU.mult), 1)
                    S.oxf.inc(v.engine_nop(), 1)

                @block.tensor
                def _(t, buf=buf):
                    S.ve.wait(t)
                    for dc in range(DCH):
                        tp_rot.acquire(t)
                        sl = tp_rot.emitted()
                        ii = t.transpose(ps_tp[:, sl * 1024:sl * 1024 + 128],
                                         xn2[buf][:, dc * 128:(dc + 1) * 128],
                                         ident[:])
                        S.te.inc(ii, 1)

                @block.vector
                def _(v, ch=ch):
                    base = S.te.n - DCH
                    for dc in range(DCH):
                        sl = (tp_rot.idx - DCH + dc) % 2
                        S.te.wait_at(v, base + dc + 1)
                        ii = v.tensor_copy(xnT[:, dc * TC + ch * C:
                                               dc * TC + (ch + 1) * C],
                                           ps_tp[:, sl * 1024:sl * 1024 + 128])
                        S.tpe.inc(ii, 1)

            if debug:
                @block.sync
                def _(sy):
                    S.tpe.wait(sy)
                    S.st.inc(sy.dma_start(out=d_xnT, in_=xnT[:]), 16)

        barrier()

        with contextlib.ExitStack() as p2b:
            sb2 = lambda n, s, d: p2b.enter_context(nc.sbuf_tensor(n, s, d))
            wbuf = [sb2(f"wbuf{i}", [128, DCH * 512], BF16) for i in range(2)]
            stg = [sb2(f"stg{i}", [128, 512], BF16) for i in range(4)]
            stg2 = [sb2(f"stg2{i}", [128, 512], BF16) for i in range(2)]
            sclip = sb2("sclip", [128, 512], BF16)
            th_sg = sb2("th_sg", [128, 512], F32)
            ps_pj = p2b.enter_context(nc.psum_tensor("ps_pj_b", [128, 1024], F32))
            ps_tp = p2b.enter_context(nc.psum_tensor("ps_tp_b", [128, 2048], BF16))
            ps_up = p2b.enter_context(nc.psum_tensor("ps_up_b", [128, 1024], F32))
            fcb2 = [sb2(f"fcb2{i}", [128, 2048], F32) for i in range(2)]
            mskb = sb2("mskb", [128, 2048], F32)
            twb2 = sb2("twb2", [128, 2048], BF16)
            wob2 = sb2("wob2", [128, DCH * 128], BF16)
            ampf = sb2("ampf", [128, FF // 128], F32)
            ampp = sb2("ampp", [128, NFT], F32)
            amtmp = sb2("amtmp", [128, 2], F32)
            pj_rot = Rot(2, S.pje)
            tp_rot = Rot(2, S.tpe)
            up_rot = Rot(2, S.upe)
            sg_pool = SlotPool(4, S.sp)
            sg2_pool = SlotPool(2, S.sp)

            def am_task(i):
                wb = i % 2

                @block.sync
                def _(sy, i=i, wb=wb):
                    S.amf.wait_at(sy, i - 1)
                    if i < FF // 128:
                        S.aml.inc(sy.dma_start(
                            out=fcb2[wb].ap().rearrange("p (c f) -> p c f", f=128),
                            in_=wfcT_h[:, i * 128:(i + 1) * 128]
                            .rearrange("(c p) f -> p c f", p=128)), 16)
                    else:
                        j = i - FF // 128
                        S.aml.inc(sy.dma_start(
                            out=fcb2[wb][:],
                            in_=wprojT_h[j * 128:(j + 1) * 128, :]), 16)

                @block.vector
                def _(v, i=i, wb=wb):
                    S.aml.wait(v)
                    dst = (ampf[:, i:i + 1] if i < FF // 128
                           else ampp[:, i - FF // 128:i - FF // 128 + 1])
                    ii = v.tensor_reduce(dst, fcb2[wb][:], AX.X, ALU.add,
                                         apply_absolute_value=True)
                    S.amf.inc(ii, 1)

            def fin_task():
                @block.vector
                def _(v):
                    S.amf.wait(v)
                    S.fzv.inc(v.tensor_reduce(amtmp[:, 0:1], ampf[:], AX.X,
                                              ALU.add), 1)
                    S.fzv.inc(v.tensor_reduce(amtmp[:, 1:2], ampp[:], AX.X,
                                              ALU.add), 1)

                @block.tensor
                def _(t):
                    S.fzv.wait(t)
                    S.fzt.inc(t.matmul(ps_up[0:1, 0:1], amtmp[:, 0:1],
                                       onesc_f32[:], start=True, stop=True,
                                       skip_group_check=True), 1)
                    S.fzt.inc(t.matmul(ps_up[0:1, 1:2], amtmp[:, 1:2],
                                       onesc_f32[:], start=True, stop=True,
                                       skip_group_check=True), 1)

                @block.vector
                def _(v):
                    S.fzt.wait(v)
                    S.fzv.inc(v.tensor_scalar(srow[0:1, 0:1], ps_up[0:1, 0:1],
                                              1.0 / (FF * D), 1e-5, ALU.mult,
                                              ALU.max), 1)
                    S.fzv.inc(v.tensor_scalar(srow[0:1, 1:2], ps_up[0:1, 1:2],
                                              1.0 / (FF * D), 1e-5, ALU.mult,
                                              ALU.max), 1)
                    S.fzv.wait(v)
                    S.fzv.inc(v.tensor_scalar(srow[0:1, 3:4], srow[0:1, 0:1],
                                              0.7, None, ALU.mult), 1)
                    S.fzv.inc(v.tensor_scalar(srow[0:1, 4:5], srow[0:1, 1:2],
                                              0.7, None, ALU.mult), 1)
                    S.fzv.wait(v)
                    S.fzv.inc(v.tensor_scalar(srow[0:1, 5:6], srow[0:1, 3:4],
                                              -1.0, None, ALU.mult), 1)
                    S.fzv.inc(v.tensor_scalar(srow[0:1, 6:7], srow[0:1, 4:5],
                                              -1.0, None, ALU.mult), 1)
                    S.fzv.inc(v.tensor_tensor(srow[0:1, 2:3], srow[0:1, 0:1],
                                              srow[0:1, 0:1], ALU.mult), 1)
                    S.fzv.wait(v)
                    S.fzv.inc(v.tensor_tensor(srow[0:1, 2:3], srow[0:1, 2:3],
                                              srow[0:1, 1:2], ALU.mult), 1)

                rows = [(srow[0:1, 3:4], thr_col[:, 0:1]),
                        (srow[0:1, 4:5], thr_col[:, 1:2]),
                        (srow[0:1, 5:6], nthr_col[:, 0:1]),
                        (srow[0:1, 6:7], nthr_col[:, 1:2]),
                        (srow[0:1, 2:3], amc_col[:, 0:1])]
                for row_ap, out_ap in rows:
                    @block.tensor
                    def _(t, row_ap=row_ap):
                        S.fzv.wait(t)
                        S.fzt.inc(t.matmul(ps_up[:, 2:3], ones1_f32[:], row_ap,
                                           start=True, stop=True,
                                           skip_group_check=True), 1)

                    @block.vector
                    def _(v, out_ap=out_ap):
                        S.fzt.wait(v)
                        ii = v.tensor_copy(out_ap, ps_up[:, 2:3])
                        S.fzv.inc(ii, 1)
                        S.fz.inc(v.engine_nop(), 1)

            def tern_task(j):
                wb = j % 2

                @block.sync
                def _(sy, j=j, wb=wb):
                    S.fz.wait_at(sy, 4)
                    S.wte2.wait_at(sy, j - 1)
                    S.tld.inc(sy.dma_start(
                        out=fcb2[wb][:],
                        in_=wprojT_h[j * 128:(j + 1) * 128, :]), 16)

                @block.vector
                def _(v, j=j, wb=wb):
                    S.tld.wait(v)
                    S.wte2.wait_at(v, j)
                    S.fzv.inc(v.tensor_scalar(mskb[:], fcb2[wb][:],
                                              nthr_col[:, 1:2], thr_col[:, 1:2],
                                              ALU.max, ALU.min), 1)
                    S.fzv.wait(v)
                    S.fzv.inc(v.tensor_tensor(mskb[:], fcb2[wb][:], mskb[:],
                                              ALU.subtract), 1)

                @block.scalar
                def _(s, j=j):
                    S.fzv.wait(s)
                    S.wsp2.wait_at(s, j * 16)
                    s.activation(twb2[:], mskb[:], AF.Sign)
                    S.wte2.inc(s.drain(), 1)

                @block.gpsimd
                def _(g, j=j):
                    S.wte2.wait(g)
                    ii = g.dma_start(
                        out=d_wpt[:, :, j * 128:(j + 1) * 128]
                        .rearrange("d p f -> p d f"),
                        in_=twb2.ap().rearrange("p (d f) -> p d f", f=128))
                    S.wsp2.inc(ii, 16)

            def wo_task(k):
                @block.gpsimd
                def _(g, k=k):
                    S.woe.wait(g)
                    S.ca.inc(g.dma_start(
                        out=wob2.ap().rearrange("p (c f) -> p c f", f=128),
                        in_=wo_h[:, k * 128:(k + 1) * 128]
                        .rearrange("(c p) f -> p c f", p=128)), 16)
                    g.wait_ge(S.ca.h, S.ca.n)
                    ii = g.dma_start(out=d_wo[k], in_=wob2[:])
                    S.woe.inc(ii, 16)

            tasks = ([lambda i=i: am_task(i) for i in range(FF // 128 + NFT)]
                     + [fin_task]
                     + [lambda j=j: tern_task(j) for j in range(NFT)]
                     + [lambda k=k: wo_task(k) for k in range(DCH)])
            task_iter = iter(tasks)

            def filler():
                try:
                    next(task_iter)()
                except StopIteration:
                    pass

            def w_load(g, w_ap, fgi, wb):
                S.wuse.wait_at(g, S.wuse.n - 1)
                S.ca.inc(g.dma_start(
                    out=wbuf[wb].ap().rearrange("p (c f) -> p c f", f=512),
                    in_=w_ap[:, fgi * 512:(fgi + 1) * 512]
                    .rearrange("(c p) f -> p c f", p=128)), 16)

            def spill_feat(dram, src_sb, gft, tt4, done_sem):
                ii_holder = []

                @block.gpsimd
                def _(g):
                    done_sem.wait(g)
                    ii = g.dma_start(
                        out=dram[tt4 * 4:(tt4 + 1) * 4, :,
                                 gft * 128:(gft + 1) * 128]
                        .rearrange("c p f -> p c f"),
                        in_=src_sb.ap().rearrange("p (c f) -> p c f", f=128))
                    S.sp.inc(ii, 16)
                    ii_holder.append(ii)
                return ii_holder

            def post_plain(dram):
                def post(gft, tt4, ps_ap, mm_mark):
                    @block.vector
                    def _(v):
                        S.te.wait_at(v, mm_mark)
                        ssl = sg_pool.acquire(v)
                        ii = v.tensor_copy(stg[ssl][:], ps_ap)
                        S.ve.inc(ii, 1)
                        S.pje.inc(v.engine_nop(), 1)
                        post.ssl = ssl

                    @block.gpsimd
                    def _(g, gft=gft, tt4=tt4):
                        S.ve.wait(g)
                        ii = g.dma_start(
                            out=dram[tt4 * 4:(tt4 + 1) * 4, :,
                                     gft * 128:(gft + 1) * 128]
                            .rearrange("c p f -> p c f"),
                            in_=stg[post.ssl].ap()
                            .rearrange("p (c f) -> p c f", f=128))
                        S.sp.inc(ii, 16)
                        sg_pool.release(post.ssl)
                return post

            def transposes_to(src_sb, dram, gft, tt4, ready):
                @block.tensor
                def _(t):
                    ready.wait(t)
                    for q in range(4):
                        tp_rot.acquire(t)
                        sl = tp_rot.emitted()
                        ii = t.transpose(ps_tp[:, sl * 1024:sl * 1024 + 128],
                                         src_sb[:, q * 128:(q + 1) * 128],
                                         ident[:])
                        S.te.inc(ii, 1)

                @block.vector
                def _(v):
                    base = S.te.n - 4
                    s2l = sg2_pool.acquire(v)
                    for q in range(4):
                        sl = (tp_rot.idx - 4 + q) % 2
                        S.te.wait_at(v, base + q + 1)
                        ii = v.tensor_copy(stg2[s2l][:, q * 128:(q + 1) * 128],
                                           ps_tp[:, sl * 1024:sl * 1024 + 128])
                        if q < 3:
                            S.tpe.inc(ii, 1)
                    S.ve.inc(ii, 1)
                    S.tpe.inc(v.engine_nop(), 1)
                    transposes_to.s2l = s2l

                @block.gpsimd
                def _(g, gft=gft, tt4=tt4):
                    S.ve.wait(g)
                    ii = g.dma_start(
                        out=dram[tt4 * 4:(tt4 + 1) * 4, :,
                                 gft * 128:(gft + 1) * 128]
                        .rearrange("c p f -> p c f"),
                        in_=stg2[transposes_to.s2l].ap()
                        .rearrange("p (c f) -> p c f", f=128))
                    S.sp.inc(ii, 16)
                    sg2_pool.release(transposes_to.s2l)

            def post_k(gft, tt4, ps_ap, mm_mark):
                post_plain(d_kT)(gft, tt4, ps_ap, mm_mark)

            def post_s(gft, tt4, ps_ap, mm_mark):
                @block.vector
                def _(v):
                    S.te.wait_at(v, mm_mark)
                    S.sev.wait(v, lag=0)
                    ii = v.tensor_scalar(sclip[:], ps_ap, -CLAMP, CLAMP,
                                         ALU.max, ALU.min)
                    S.ve.inc(ii, 1)
                    S.pje.inc(v.engine_nop(), 1)

                @block.scalar
                def _(s):
                    S.ve.wait(s)
                    ssl = sg_pool.acquire(s)
                    ii = s.activation(stg[ssl][:], sclip[:], AF.Exp)
                    S.sev.inc(ii, 1)
                    post_s.ssl = ssl

                @block.gpsimd
                def _(g, gft=gft, tt4=tt4):
                    S.sev.wait(g)
                    ii = g.dma_start(
                        out=d_ef[tt4 * 4:(tt4 + 1) * 4, :,
                                 gft * 128:(gft + 1) * 128]
                        .rearrange("c p f -> p c f"),
                        in_=stg[post_s.ssl].ap()
                        .rearrange("p (c f) -> p c f", f=128))
                    S.sp.inc(ii, 16)

                transposes_to(stg[post_s.ssl][:], d_et, gft, tt4, S.sev)

                @block.gpsimd
                def _(g):
                    S.te.wait(g)
                    S.sp.wait(g)
                    sg_pool.release(post_s.ssl)
                    S.sge.inc(g.engine_nop(), 1)

            def sweep_feat(w_ap, FW, post):
                for fgi in range(FW // 512):
                    wb = fgi % 2

                    @block.gpsimd
                    def _(g, fgi=fgi, wb=wb):
                        w_load(g, w_ap, fgi, wb)

                    for tt4 in range(NT4):
                        for ft in range(4):
                            gft = fgi * 4 + ft

                            @block.tensor
                            def _(t, wb=wb, tt4=tt4, ft=ft):
                                S.ca.wait(t)
                                pj_rot.acquire(t)
                                sl = pj_rot.emitted()
                                for dc in range(DCH):
                                    ii = t.matmul(
                                        ps_pj[:, sl * 512:(sl + 1) * 512],
                                        wbuf[wb][:, dc * 512 + ft * 128:
                                                 dc * 512 + (ft + 1) * 128],
                                        xnT[:, dc * TC + tt4 * 512:
                                            dc * TC + (tt4 + 1) * 512],
                                        start=(dc == 0), stop=(dc == DCH - 1))
                                S.te.inc(ii, 1)
                                if tt4 == NT4 - 1 and ft == 3:
                                    S.wuse.inc(t.drain(), 1)

                            sl = (pj_rot.idx - 1) % 2
                            post(gft, tt4,
                                 ps_pj[:, sl * 512:(sl + 1) * 512], S.te.n)
                            filler()

            def sweep_tok(w_ap, FW, dram, silu):
                sweep_tok.thmark = None
                for fgi in range(FW // 512):
                    wb = fgi % 2

                    @block.gpsimd
                    def _(g, fgi=fgi, wb=wb):
                        w_load(g, w_ap, fgi, wb)

                    for ch in range(NOWN):
                        @block.tensor
                        def _(t, wb=wb, ch=ch):
                            S.ca.wait(t)
                            pj_rot.acquire(t)
                            sl = pj_rot.emitted()
                            for dc in range(DCH):
                                ii = t.matmul(
                                    ps_pj[:, sl * 512:(sl + 1) * 512],
                                    xnT[:, dc * TC + ch * C:dc * TC + (ch + 1) * C],
                                    wbuf[wb][:, dc * 512:(dc + 1) * 512],
                                    start=(dc == 0), stop=(dc == DCH - 1))
                            S.te.inc(ii, 1)
                            if ch == NOWN - 1:
                                S.wuse.inc(t.drain(), 1)

                        sl = (pj_rot.idx - 1) % 2
                        mm_mark = S.te.n
                        if silu:
                            @block.scalar
                            def _(s, ch=ch, sl=sl, mm_mark=mm_mark):
                                S.te.wait_at(s, mm_mark)
                                if sweep_tok.thmark is not None:
                                    S.th.wait_at(s, sweep_tok.thmark)
                                ssl = sg_pool.acquire(s)
                                ii = s.activation(
                                    th_sg[:],
                                    ps_pj[:, sl * 512:(sl + 1) * 512],
                                    AF.Tanh, scale=0.5)
                                S.sle.inc(ii, 1)
                                sweep_tok.ssl = ssl

                            @block.vector
                            def _(v, sl=sl):
                                S.sle.wait(v)
                                ii = v.scalar_tensor_tensor(
                                    stg[sweep_tok.ssl][:], th_sg[:], 1.0,
                                    ps_pj[:, sl * 512:(sl + 1) * 512],
                                    ALU.add, ALU.mult)
                                S.th.inc(ii, 1)
                                sweep_tok.thmark = S.th.n
                                S.pje.inc(v.engine_nop(), 1)

                            @block.gpsimd
                            def _(g, fgi=fgi, ch=ch):
                                S.th.wait(g)
                                ii = g.dma_start(
                                    out=dram[ch, :, fgi * 512:(fgi + 1) * 512],
                                    in_=stg[sweep_tok.ssl][:])
                                S.sp.inc(ii, 16)
                                sg_pool.release(sweep_tok.ssl)
                        else:
                            @block.vector
                            def _(v, ch=ch, sl=sl, mm_mark=mm_mark):
                                S.te.wait_at(v, mm_mark)
                                ssl = sg_pool.acquire(v)
                                ii = v.tensor_copy(
                                    stg[ssl][:],
                                    ps_pj[:, sl * 512:(sl + 1) * 512])
                                S.ve.inc(ii, 1)
                                S.pje.inc(v.engine_nop(), 1)
                                sweep_tok.ssl = ssl

                            @block.gpsimd
                            def _(g, fgi=fgi, ch=ch):
                                S.ve.wait(g)
                                ii = g.dma_start(
                                    out=dram[ch, :, fgi * 512:(fgi + 1) * 512],
                                    in_=stg[sweep_tok.ssl][:])
                                S.sp.inc(ii, 16)
                                sg_pool.release(sweep_tok.ssl)
                        filler()

            def sk0_from_x(fgi, wb):
                for j in range(8):
                    h = fgi * 8 + j

                    @block.tensor
                    def _(t, h=h, j=j, wb=wb):
                        S.ca.wait(t)
                        up_rot.acquire(t)
                        usl = up_rot.emitted()
                        for dc in range(DCH):
                            ii = t.matmul(
                                ps_up[0:M, usl * 512:usl * 512 + M],
                                wbuf[wb][:, dc * 512 + j * 64:
                                         dc * 512 + (j + 1) * 64],
                                X_bf[:, dc * MD + h * M:dc * MD + (h + 1) * M],
                                start=(dc == 0), stop=(dc == DCH - 1),
                                skip_group_check=True)
                        S.te.inc(ii, 1)
                        sk0_from_x.usl = usl

                    @block.vector
                    def _(v, h=h):
                        usl = sk0_from_x.usl
                        S.te.wait(v)
                        pb = (h % 2) * 64
                        ii = v.tensor_copy(
                            SKs[pb:pb + M, (h // 2) * M:(h // 2 + 1) * M],
                            ps_up[0:M, usl * 512:usl * 512 + M])
                        S.upe.inc(ii, 1)
                        S.ve.inc(v.engine_nop(), 1)

            def sv0_from_x(fgi, wb):
                for j in range(4):
                    h = fgi * 4 + j

                    @block.tensor
                    def _(t, h=h, j=j, wb=wb):
                        S.ca.wait(t)
                        up_rot.acquire(t)
                        usl = up_rot.emitted()
                        for dc in range(DCH):
                            ii = t.matmul(
                                ps_up[0:M, usl * 512:usl * 512 + DV],
                                X_bf[:, dc * MD + h * M:dc * MD + (h + 1) * M],
                                wbuf[wb][:, dc * 512 + j * 128:
                                         dc * 512 + (j + 1) * 128],
                                start=(dc == 0), stop=(dc == DCH - 1),
                                skip_group_check=True)
                        S.te.inc(ii, 1)
                        sv0_from_x.usl = usl

                    @block.vector
                    def _(v, h=h):
                        usl = sv0_from_x.usl
                        S.te.wait(v)
                        pb = (h % 2) * 64
                        ii = v.tensor_copy(
                            SVs[pb:pb + M, (h // 2) * DV:(h // 2 + 1) * DV],
                            ps_up[0:M, usl * 512:usl * 512 + DV])
                        S.upe.inc(ii, 1)
                        S.ve.inc(v.engine_nop(), 1)

            sweep_feat(wk_h, KD, post_k)
            sweep_feat(ws_h, MD, post_s)
            sweep_feat(wq_h, KD, post_plain(d_q))
            sweep_tok(wv_h, VD, d_v, False)
            sweep_tok(wg_h, VD, d_sg, True)
            for _t in task_iter:
                _t()

            @block.scalar
            def _(s):
                act_set(s, 6)
            barrier()

            for fgi in range(KD // 512):
                wb = fgi % 2

                @block.gpsimd
                def _(g, fgi=fgi, wb=wb):
                    S.upe.wait(g)
                    w_load(g, wk_h, fgi, wb)
                sk0_from_x(fgi, wb)
            for fgi in range(VD // 512):
                wb = fgi % 2

                @block.gpsimd
                def _(g, fgi=fgi, wb=wb):
                    S.upe.wait(g)
                    w_load(g, wv_h, fgi, wb)
                sv0_from_x(fgi, wb)

            if debug:
                @block.sync
                def _(sy):
                    S.ve.wait(sy)
                    S.st.inc(sy.dma_start(out=d_SK, in_=SKs[:]), 16)
                    S.st.inc(sy.dma_start(out=d_SV, in_=SVs[:]), 16)

        barrier()

        with contextlib.ExitStack() as p2c:
            sb2 = lambda n, s, d: p2c.enter_context(nc.sbuf_tensor(n, s, d))
            fcb = [sb2(f"fcb{i}", [128, DCH * 256], F32) for i in range(2)]
            msk = sb2("msk", [128, DCH * 256], F32)
            twb = sb2("twb", [128, DCH * 256], BF16)
            hr_st = [sb2(f"hr{i}", [128, 512], BF16) for i in range(2)]
            ps_pj = p2c.enter_context(nc.psum_tensor("ps_pj_c", [128, 1024], F32))
            hr_rot = Rot(2, S.hre)
            sev_rot = Rot(2, S.sev)
            NSL_FC = FF // 256

            for fs in range(NSL_FC):
                wb = fs % 2

                @block.sync
                def _(sy, fs=fs, wb=wb):
                    S.twe.wait_at(sy, fs - 1)
                    S.ld.inc(sy.dma_start(
                        out=fcb[wb].ap().rearrange("p (c f) -> p c f", f=256),
                        in_=wfcT_h[:, fs * 256:(fs + 1) * 256]
                        .rearrange("(c p) f -> p c f", p=128)), 16)

                @block.vector
                def _(v, fs=fs, wb=wb):
                    S.ld.wait(v)
                    S.twe.wait(v)
                    S.ve.inc(v.tensor_scalar(msk[:], fcb[wb][:],
                                             nthr_col[:, 0:1],
                                             thr_col[:, 0:1], ALU.max,
                                             ALU.min), 1)
                    S.ve.wait(v)
                    S.ve.inc(v.tensor_tensor(msk[:], fcb[wb][:], msk[:],
                                             ALU.subtract), 1)

                @block.scalar
                def _(s):
                    S.ve.wait(s)
                    S.te.wait(s)
                    s.activation(twb[:], msk[:], AF.Sign)
                    S.twe.inc(s.drain(), 1)

                for tt4 in range(NT4):
                    for ft in range(2):
                        gft = fs * 2 + ft

                        @block.tensor
                        def _(t, tt4=tt4, ft=ft):
                            S.twe.wait(t)
                            sev_rot.acquire(t)
                            sl = sev_rot.emitted()
                            for dc in range(DCH):
                                ii = t.matmul(
                                    ps_pj[:, sl * 512:(sl + 1) * 512],
                                    twb[:, dc * 256 + ft * 128:
                                        dc * 256 + (ft + 1) * 128],
                                    xnT[:, dc * TC + tt4 * 512:
                                        dc * TC + (tt4 + 1) * 512],
                                    start=(dc == 0), stop=(dc == DCH - 1))
                            S.te.inc(ii, 1)

                        sl = (sev_rot.idx - 1) % 2
                        mm_mark = S.te.n
                        hsl_now = hr_rot.emitted()

                        @block.vector
                        def _(v, sl=sl, mm_mark=mm_mark, hsl=hsl_now):
                            S.te.wait_at(v, mm_mark)
                            S.sp16.wait_at(v, S.sp16.n - 16)
                            ii = v.tensor_scalar(
                                hr_st[hsl][:],
                                ps_pj[:, sl * 512:(sl + 1) * 512], 0.0,
                                None, ALU.max)
                            S.sev.inc(ii, 1)
                            ii = v.tensor_tensor(hr_st[hsl][:], hr_st[hsl][:],
                                                 hr_st[hsl][:], ALU.mult)
                            S.hre.inc(ii, 1)

                        @block.gpsimd
                        def _(g, gft=gft, tt4=tt4, hsl=hsl_now):
                            S.hre.wait(g)
                            ii = g.dma_start(
                                out=d_hr[gft, :, tt4 * 512:(tt4 + 1) * 512],
                                in_=hr_st[hsl][:])
                            S.sp16.inc(ii, 16)

        barrier()

        p12.close()
        barrier()

        with contextlib.ExitStack() as p3:
            sb3 = lambda n, s, d: p3.enter_context(nc.sbuf_tensor(n, s, d))
            rq = [sb3(f"rq{i}", [128, KD], BF16) for i in range(2)]
            rkT = [sb3(f"rkT{i}", [128, KD], BF16) for i in range(2)]
            ktok = sb3("ktok", [128, KD], BF16)
            ref_ = [sb3(f"ref{i}", [128, MD], BF16) for i in range(2)]
            ret = [sb3(f"ret{i}", [128, MD], BF16) for i in range(2)]
            rv = [sb3(f"rv{i}", [128, VD], BF16) for i in range(2)]
            rsg = [sb3(f"rsg{i}", [128, VD], BF16) for i in range(2)]
            hr_g = sb3("hr_g", [128, NFT * 512], BF16)
            wptb = [sb3(f"wptb{i}", [128, NFT * 128], BF16) for i in range(2)]
            wobuf = [sb3(f"wobuf{i}", [128, DCH * 128], BF16) for i in range(2)]
            xat = sb3("xat", [128, DCH * 512], F32)
            ogT_g = sb3("ogT_g", [128, DCH * 512], BF16)
            a_sb = [sb3(f"a_sb{i}", [128, MD], F32) for i in range(2)]
            okr = [sb3(f"okr{i}", [128, MD], F32) for i in range(2)]
            rA = sb3("rA", [128, MD], F32)
            u_bf = sb3("u_bf", [128, MD], BF16)
            ut_bf = sb3("ut_bf", [128, 8 * 128], BF16)
            og_bf = sb3("og_bf", [128, VD], BF16)
            ov_sb = sb3("ov_sb", [128, VD], BF16)
            ptm = [sb3(f"ptm{i}", [128, 128], BF16) for i in range(2)]
            w2m = [sb3(f"w2m{i}", [128, 128], BF16) for i in range(2)]
            t1g = sb3("t1g", [128, DV], F32)
            mx16 = sb3("mx16", [128, 16], F32)
            nmx16 = sb3("nmx16", [128, 16], F32)
            sume = sb3("sume", [128, 16], F32)
            rsum = sb3("rsum", [128, 16], F32)
            ssq = sb3("ssq", [128, 64], F32)
            rstd_o = sb3("rstd_o", [128, 64], F32)
            skb3 = [sb3(f"skb3{i}", [128, 8 * M], BF16) for i in range(3)]
            svb3 = [sb3(f"svb3{i}", [128, 8 * DV], BF16) for i in range(3)]

            ps_ok = p3.enter_context(nc.psum_tensor("ps_ok", [128, 1024], F32))
            ps_ai = p3.enter_context(nc.psum_tensor("ps_ai", [128, 1024], F32))
            ps_pt = p3.enter_context(nc.psum_tensor("ps_pt", [128, 1024], F32))
            ps_tp = p3.enter_context(nc.psum_tensor("ps_tp_3", [128, 2048], BF16))
            ps_ov = ps_pt
            tp_rot = Rot(2, S.tpe)
            marks = {"ut": 0, "og": 0, "rA": {}, "okn": {}}

            @block.sync
            def _(sy):
                S.sp.wait(sy)
                S.sp16.wait(sy)
                S.woe.wait(sy)
                S.wsp2.wait(sy)
                S.spg.inc(sy.nop(), 1)

            @block.gpsimd
            def _(g):
                S.spg.wait(g)

            @block.vector
            def _(v):
                v.memset(ssq[:], 1.0)

            def ring_load(ch):
                sl = ch % 2

                @block.sync
                def _(sy, ch=ch, sl=sl):
                    S.gate.wait_at(sy, (ch - 1) * H)
                    for dram, buf in ((d_q, rq), (d_kT, rkT),
                                      (d_ef, ref_), (d_et, ret),
                                      (d_v, rv), (d_sg, rsg)):
                        S.rld.inc(sy.dma_start(out=buf[sl][:], in_=dram[ch]), 16)

            def fslice(buf, h, w):
                pb = (h % 2) * 64
                return buf[pb:pb + 64, (h // 2) * w:(h // 2) * w + w]

            def phase1(ch):
                sl = ch % 2
                q, kT, et, v_ = rq[sl], rkT[sl], ret[sl], rv[sl]
                kt = ktok
                SKb_c = skb3[ch % 3]

                te_kt_base = S.te.n

                @block.tensor
                def _(t):
                    S.rld.wait(t)
                    for j in range(8):
                        tp_rot.acquire(t)
                        tsl = tp_rot.emitted()
                        ii = t.transpose(ps_tp[:, tsl * 1024:tsl * 1024 + 128],
                                         kT[:, j * 128:(j + 1) * 128], ident[:])
                        S.te.inc(ii, 1)

                @block.vector
                def _(v):
                    for j in range(8):
                        tsl = (tp_rot.idx - 8 + j) % 2
                        S.te.wait_at(v, te_kt_base + j + 1)
                        ii = v.tensor_copy(ktok[:, j * 128:(j + 1) * 128],
                                           ps_tp[:, tsl * 1024:tsl * 1024 + 128])
                        S.tpe.inc(ii, 1)

                @block.tensor
                def _(t):
                    S.stup.wait(t)
                    S.stu.wait(t)
                    S.te.inc(t.matmul(ps_ai[:, 0:512], ones1_f32[:],
                                      A_row[0:1, 0:512], start=True, stop=False,
                                      skip_group_check=True), 1)
                    S.te.inc(t.matmul(ps_ai[:, 512:1024], ones1_f32[:],
                                      A_row[0:1, 512:MD], start=True, stop=False,
                                      skip_group_check=True), 1)

                te_pt_base = S.te.n
                ptm_base = S.ptme.n

                @block.tensor
                def _(t):
                    S.rld.wait(t)
                    S.ove.wait(t)
                    for h in range(H):
                        if h >= 2:
                            S.ptme.wait_at(t, ptm_base + h - 1)
                        psl = h % 2
                        ii = t.matmul(ps_pt[:, psl * 512:psl * 512 + 128],
                                      fslice(kT, h, 128), fslice(q, h, 128),
                                      start=True, stop=True,
                                      skip_group_check=True)
                        S.te.inc(ii, 1)

                @block.vector
                def _(v):
                    for h in range(H):
                        psl = h % 2
                        S.te.wait_at(v, te_pt_base + h + 1)
                        ii = v.tensor_tensor(ptm[psl][:],
                                             ps_pt[:, psl * 512:psl * 512 + 128],
                                             mask01[:], ALU.mult)
                        S.ptme.inc(ii, 1)

                @block.tensor
                def _(t, ch=ch):
                    S.stc.wait(t)
                    S.okf.wait(t)
                    for h in range(H):
                        S.ptme.wait_at(t, ptm_base + h + 1)
                        psl = h % 2
                        t.matmul(ps_ok[:, h * 64:(h + 1) * 64], ptm[psl][:],
                                 et[:, h * 64:(h + 1) * 64], start=True,
                                 stop=False, skip_group_check=True)
                        ii = t.matmul(ps_ok[:, h * 64:(h + 1) * 64],
                                      fslice(q, h, 128), fslice(SKb_c, h, 64),
                                      start=False, stop=True,
                                      skip_group_check=True)
                        S.te.inc(ii, 1)
                    for h in range(H):
                        ii = t.matmul(ps_ai[:, h * 64:(h + 1) * 64], mask01[:],
                                      et[:, h * 64:(h + 1) * 64], start=False,
                                      stop=(h % 8 == 7), skip_group_check=True)
                    S.te.inc(ii, 1)

                @block.vector
                def _(v, ch=ch):
                    S.te.wait(v)
                    ii = v.tensor_copy(okr[ch % 2][:], ps_ok[:])
                    S.okf.inc(ii, 1)
                    ii = v.tensor_copy(a_sb[ch % 2][:], ps_ai[:])
                    S.ve.inc(ii, 1)
                    S.aie.inc(v.engine_nop(), 1)

                stup_base = S.stup.n

                @block.tensor
                def _(t, ch=ch):
                    S.aie.wait(t)
                    for h in range(H):
                        usl = h % 2
                        S.stup.wait_at(t, stup_base + h - 1)
                        t.matmul(ps_ai[0:64, usl * 512 + 0:usl * 512 + 64],
                                 kt[:, h * 64:(h + 1) * 64],
                                 et[:, h * 64:(h + 1) * 64],
                                 start=True, stop=True, skip_group_check=True)
                        t.matmul(ps_ai[0:64, usl * 512 + 128:usl * 512 + 256],
                                 et[:, h * 64:(h + 1) * 64],
                                 v_[:, h * 128:(h + 1) * 128],
                                 start=True, stop=True, skip_group_check=True)
                        ii = t.matmul(ps_ai[0:1, usl * 512 + 64:usl * 512 + 128],
                                      ones_bf[:, 0:1],
                                      et[:, h * 64:(h + 1) * 64],
                                      start=True, stop=True,
                                      skip_group_check=True)
                        S.te.inc(ii, 1)

                @block.vector
                def _(v, ch=ch):
                    base = S.te.n - H
                    for h in range(H):
                        usl = h % 2
                        S.te.wait_at(v, base + h + 1)
                        pb = (h % 2) * 64
                        v.tensor_tensor(
                            SKs[pb:pb + 64, (h // 2) * M:(h // 2 + 1) * M],
                            SKs[pb:pb + 64, (h // 2) * M:(h // 2 + 1) * M],
                            ps_ai[0:64, usl * 512 + 0:usl * 512 + 64], ALU.add)
                        v.tensor_tensor(
                            SVs[pb:pb + 64, (h // 2) * DV:(h // 2 + 1) * DV],
                            SVs[pb:pb + 64, (h // 2) * DV:(h // 2 + 1) * DV],
                            ps_ai[0:64, usl * 512 + 128:usl * 512 + 256],
                            ALU.add)
                        ii = v.tensor_tensor(
                            A_row[0:1, h * 64:(h + 1) * 64],
                            A_row[0:1, h * 64:(h + 1) * 64],
                            ps_ai[0:1, usl * 512 + 64:usl * 512 + 128], ALU.add)
                        S.stup.inc(ii, 1)
                    S.stu.inc(v.engine_nop(), 1)
                    S.ve.wait(v)
                    S.ove.wait(v)
                    v.tensor_copy(skb3[(ch + 1) % 3][:], SKs[:])
                    ii = v.tensor_copy(svb3[(ch + 1) % 3][:], SVs[:])
                    S.stc.inc(ii, 1)

            def phase2(ch, g):
                sl = ch % 2
                q, kT, ef, v_, sg = rq[sl], rkT[sl], ref_[sl], rv[sl], rsg[sl]
                SVb_c = svb3[ch % 3]

                @block.vector
                def _(v, ch=ch):
                    S.se.wait_at(v, se_b3 + ch)
                    ok_ = okr[ch % 2]
                    v.reciprocal(rA[:], a_sb[ch % 2][:])
                    v.tensor_tensor(ok_[:, 0:512], ok_[:, 0:512],
                                    rA[:, 0:512], ALU.mult)
                    ii = v.tensor_tensor(ok_[:, 512:MD], ok_[:, 512:MD],
                                         rA[:, 512:MD], ALU.mult)
                    S.ve.inc(ii, 1)
                    S.ve.wait(v)
                    S.ve.inc(v.tensor_reduce(
                        mx16[:], ok_.ap().rearrange("p (h m) -> p h m", m=64),
                        AX.X, ALU.max), 1)
                    S.ve.wait(v)
                    S.ve.inc(v.tensor_scalar(nmx16[:], mx16[:], -SCALE, None,
                                             ALU.mult), 1)

                p3_sc_jobs.append(("exps", S.ve.n, ch))

                @block.vector
                def _(v, ch=ch):
                    S.se.wait_at(v, se_b3 + ch + 1)
                    ok_ = okr[ch % 2]
                    S.ve.inc(v.reciprocal(rsum[:], sume[:]), 1)
                    S.ve.wait(v)
                    S.tpe.wait_at(v, marks["ut"])
                    for h in range(H):
                        ii = v.scalar_tensor_tensor(
                            u_bf[:, h * 64:(h + 1) * 64],
                            ok_[:, h * 64:(h + 1) * 64], rsum[:, h:h + 1],
                            rA[:, h * 64:(h + 1) * 64], ALU.mult, ALU.mult)
                    S.ve.inc(ii, 1)

                te_ut_base = S.te.n

                @block.tensor
                def _(t):
                    S.ve.wait(t)
                    for j in range(8):
                        tp_rot.acquire(t)
                        tsl = tp_rot.emitted()
                        ii = t.transpose(ps_tp[:, tsl * 1024:tsl * 1024 + 128],
                                         u_bf[:, j * 128:(j + 1) * 128], ident[:])
                        S.te.inc(ii, 1)

                @block.vector
                def _(v):
                    S.ove.wait(v)
                    for j in range(8):
                        tsl = (tp_rot.idx - 8 + j) % 2
                        S.te.wait_at(v, te_ut_base + j + 1)
                        ii = v.tensor_copy(ut_bf[:, j * 128:(j + 1) * 128],
                                           ps_tp[:, tsl * 1024:tsl * 1024 + 128])
                        S.tpe.inc(ii, 1)
                    marks["ut"] = S.tpe.n

                te_w2_base = S.te.n
                w2m_base = S.w2me.n

                @block.tensor
                def _(t):
                    S.ve.wait(t)
                    S.ptme.wait(t)
                    S.ove.wait(t)
                    for h in range(H):
                        if h >= 2:
                            S.w2me.wait_at(t, w2m_base + h - 1)
                        psl = h % 2
                        ii = t.matmul(
                            ps_pt[:, psl * 512:psl * 512 + 128],
                            fslice(ef, h, 128),
                            ut_bf[(h % 2) * 64:(h % 2) * 64 + 64,
                                  (h // 2) * 128:(h // 2 + 1) * 128],
                            start=True, stop=True, skip_group_check=True)
                        S.te.inc(ii, 1)

                @block.vector
                def _(v):
                    for h in range(H):
                        psl = h % 2
                        S.te.wait_at(v, te_w2_base + h + 1)
                        ii = v.tensor_tensor(w2m[psl][:],
                                             ps_pt[:, psl * 512:psl * 512 + 128],
                                             mask01[:], ALU.mult)
                        S.w2me.inc(ii, 1)

                ove_base = S.ove.n
                te_ov_base = S.te.n

                @block.tensor
                def _(t):
                    S.w2me.wait(t)
                    for h in range(H):
                        if h >= 2:
                            S.ove.wait_at(t, ove_base + h - 1)
                        psl = h % 2
                        t.matmul(ps_ov[:, psl * 512:psl * 512 + 128],
                                 w2m[psl][:],
                                 v_[:, h * 128:(h + 1) * 128], start=True,
                                 stop=False, skip_group_check=True)
                        ii = t.matmul(ps_ov[:, psl * 512:psl * 512 + 128],
                                      ut_bf[(h % 2) * 64:(h % 2) * 64 + 64,
                                            (h // 2) * 128:(h // 2 + 1) * 128],
                                      fslice(SVb_c, h, 128), start=False,
                                      stop=True, skip_group_check=True)
                        S.te.inc(ii, 1)

                @block.vector
                def _(v):
                    for h in range(H):
                        psl = h % 2
                        S.te.wait_at(v, te_ov_base + h + 1)
                        ii = v.tensor_copy(ov_sb[:, h * 128:(h + 1) * 128],
                                           ps_ov[:, psl * 512:psl * 512 + 128])
                        S.ove.inc(ii, 1)

                p3_sc_jobs.append(("squares", S.ove.n, ch))

                @block.vector
                def _(v, ch=ch):
                    S.sqe.wait_at(v, sqe_b3 + (ch + 1) * H)
                    v.tensor_scalar(qk_a[:, 0:64], ssq[:], 1.0 / DV, EPS,
                                    ALU.mult, ALU.add)
                    rsqrt_dve(v, rstd_o[:], qk_a[:, 0:64], qk_b[:, 0:64], 64)
                    S.tpe.wait_at(v, marks["og"])
                    for h in range(H):
                        v.scalar_tensor_tensor(
                            t1g[:], ov_sb[:, h * 128:(h + 1) * 128],
                            rstd_o[:, h:h + 1], gnw_bc[:], ALU.mult, ALU.mult)
                        S.gate.inc(v.tensor_tensor(
                            og_bf[:, h * 128:(h + 1) * 128], t1g[:],
                            sg[:, h * 128:(h + 1) * 128], ALU.mult), 1)

                te_og_base = S.te.n

                @block.tensor
                def _(t):
                    S.gate.wait(t)
                    for kc in range(DCH):
                        tp_rot.acquire(t)
                        tsl = tp_rot.emitted()
                        ii = t.transpose(ps_tp[:, tsl * 1024:tsl * 1024 + 128],
                                         og_bf[:, kc * 128:(kc + 1) * 128],
                                         ident[:])
                        S.te.inc(ii, 1)

                @block.vector
                def _(v, ch=ch, g=g):
                    cq = ch % 4
                    S.woev.wait_at(v, g * DCH)
                    for kc in range(DCH):
                        tsl = (tp_rot.idx - DCH + kc) % 2
                        S.te.wait_at(v, te_og_base + kc + 1)
                        ii = v.tensor_copy(
                            ogT_g[:, kc * 512 + cq * 128:
                                  kc * 512 + (cq + 1) * 128],
                            ps_tp[:, tsl * 1024:tsl * 1024 + 128])
                        if kc < DCH - 1:
                            S.tpe.inc(ii, 1)
                    S.oge.inc(ii, 1)
                    S.tpe.inc(v.engine_nop(), 1)
                    marks["og"] = S.tpe.n

            def group_loads(g):
                @block.sync
                def _(sy, g=g):
                    S.mrg.wait_at(sy, g * DCH)
                    S.st.wait(sy)
                    for half in range(2):
                        ii = sy.dma_start(
                            out=hr_g.ap().rearrange("p (f t) -> p f t", t=512)
                            [:, half * (NFT // 2):(half + 1) * (NFT // 2), :],
                            in_=d_hr[half * (NFT // 2):(half + 1) * (NFT // 2),
                                     :, g * 512:(g + 1) * 512]
                            .rearrange("f p t -> p f t"))
                        S.hrl.inc(ii, 16)
                    ii = sy.dma_start(
                        out=xat.ap().rearrange("p (c t) -> p c t", t=512),
                        in_=xownT[:, g * 512:(g + 1) * 512]
                        .rearrange("(c p) t -> p c t", p=128))
                    S.xtl.inc(ii, 16)

                @block.gpsimd
                def _(gp, g=g):
                    for dsl in range(DCH):
                        wb = dsl % 2
                        S.woev.wait_at(gp, g * DCH + dsl - 1)
                        S.wol.inc(gp.dma_start(out=wobuf[wb][:],
                                               in_=d_wo[dsl]), 16)
                    for dsl in range(DCH):
                        wb = dsl % 2
                        S.mrg.wait_at(gp, g * DCH + dsl - 1)
                        S.wpl.inc(gp.dma_start(out=wptb[wb][:],
                                               in_=d_wpt[dsl]), 16)

            def emit_proj_piece(g, dsl):
                @block.tensor
                def _(t, g=g, dsl=dsl):
                    wb = dsl % 2
                    if dsl == 0:
                        S.hrl.wait_at(t, (g + 1) * 32)
                    S.wpl.wait_at(t, (g * DCH + dsl + 1) * 16)
                    S.mrg.wait_at(t, g * DCH + dsl - 1)
                    S.woev.wait(t)
                    S.okf.wait(t)
                    psl = dsl % 2
                    for fc2 in range(NFT):
                        ii = t.matmul(ps_ok[:, psl * 512:(psl + 1) * 512],
                                      wptb[wb][:, fc2 * 128:(fc2 + 1) * 128],
                                      hr_g[:, fc2 * 512:(fc2 + 1) * 512],
                                      start=(fc2 == 0),
                                      stop=(fc2 == NFT - 1))
                    S.te.inc(ii, 1)

                @block.vector
                def _(v, g=g, dsl=dsl):
                    S.te.wait_at(v, S.te.n)
                    S.xtl.wait_at(v, (g + 1) * 16)
                    psl = dsl % 2
                    ii = v.scalar_tensor_tensor(
                        xat[:, dsl * 512:(dsl + 1) * 512],
                        ps_ok[:, psl * 512:(psl + 1) * 512],
                        amc_col[:, 0:1],
                        xat[:, dsl * 512:(dsl + 1) * 512],
                        ALU.mult, ALU.add)
                    S.mrg.inc(ii, 1)
                    S.okf.inc(v.engine_nop(), 1)

            def emit_group_epilogue(g):
                @block.tensor
                def _(t, g=g):
                    S.oge.wait(t)
                    S.okf.wait(t)
                    for dsl in range(DCH):
                        wb = dsl % 2
                        S.wol.wait_at(t, (g * DCH + dsl + 1) * 16)
                        S.woev.wait_at(t, g * DCH + dsl - 1)
                        psl = dsl % 2
                        for kc in range(DCH):
                            ii = t.matmul(ps_ok[:, psl * 512:(psl + 1) * 512],
                                          wobuf[wb][:, kc * 128:(kc + 1) * 128],
                                          ogT_g[:, kc * 512:(kc + 1) * 512],
                                          start=(kc == 0), stop=(kc == DCH - 1))
                        S.te.inc(ii, 1)

                @block.vector
                def _(v, g=g):
                    base = S.te.n - DCH
                    S.xtl.wait(v)
                    for dsl in range(DCH):
                        psl = dsl % 2
                        S.te.wait_at(v, base + dsl + 1)
                        ii = v.tensor_tensor(xat[:, dsl * 512:(dsl + 1) * 512],
                                             xat[:, dsl * 512:(dsl + 1) * 512],
                                             ps_ok[:, psl * 512:(psl + 1) * 512],
                                             ALU.add)
                        S.woev.inc(ii, 1)
                    S.okf.inc(v.engine_nop(), 1)

            def emit_store(g):
                @block.sync
                def _(sy, g=g):
                    S.mrg.wait(sy)
                    S.woev.wait(sy)
                    ii = sy.dma_start(
                        out=outT_h[:, g * 512:(g + 1) * 512]
                        .rearrange("(c p) t -> p c t", p=128),
                        in_=xat.ap().rearrange("p (c t) -> p c t", t=512))
                    S.st.inc(ii, 16)

            @block.vector
            def _(v):
                v.tensor_copy(skb3[0][:], SKs[:])
                ii = v.tensor_copy(svb3[0][:], SVs[:])
                S.stc.inc(ii, 1)
                S.stu.inc(v.engine_nop(), 1)
                S.okf.inc(v.engine_nop(), 1)

            INTERLEAVE_PJ = False
            se_b3 = S.se.n
            sqe_b3 = S.sqe.n
            p3_sc_jobs = []
            group_loads(0)
            for ch in range(NOWN):
                ring_load(ch)
                phase1(ch)
                if ch > 0:
                    phase2(ch - 1, (ch - 1) // 4)
                    if INTERLEAVE_PJ:
                        g_p, i0 = divmod(ch - 1, 4)
                        for dsl in range(i0 * 4, i0 * 4 + 4):
                            emit_proj_piece(g_p, dsl)
                    if (ch - 1) % 4 == 3:
                        g = (ch - 1) // 4
                        emit_group_epilogue(g)
                        if not INTERLEAVE_PJ:
                            for dsl in range(DCH):
                                emit_proj_piece(g, dsl)
                        emit_store(g)
                        if g + 1 < NGRP:
                            group_loads(g + 1)
            phase2(NOWN - 1, (NOWN - 1) // 4)
            if INTERLEAVE_PJ:
                g_p, i0 = divmod(NOWN - 1, 4)
                for dsl in range(i0 * 4, i0 * 4 + 4):
                    emit_proj_piece(g_p, dsl)
                emit_group_epilogue(NGRP - 1)
            else:
                emit_group_epilogue(NGRP - 1)
                for dsl in range(DCH):
                    emit_proj_piece(NGRP - 1, dsl)
            emit_store(NGRP - 1)

            @block.scalar
            def _(s):
                for kind, m, ch in p3_sc_jobs:
                    ok_ = okr[ch % 2]
                    if kind == "exps":
                        S.ve.wait_at(s, m)
                        for h in range(H):
                            ii = s.activation(ok_[:, h * 64:(h + 1) * 64],
                                              ok_[:, h * 64:(h + 1) * 64],
                                              AF.Exp, scale=SCALE,
                                              bias=nmx16[:, h:h + 1],
                                              accum_out=sume[:, h:h + 1])
                        S.se.inc(ii, 1)
                    else:
                        S.ove.wait_at(s, m)
                        for h in range(H):
                            ii = s.activation(ok_[:, 0:128],
                                              ov_sb[:, h * 128:(h + 1) * 128],
                                              AF.Square,
                                              accum_out=ssq[:, h:h + 1])
                            S.sqe.inc(ii, 1)

            @block.sync
            def _(sy):
                S.st.wait(sy)
                S.sp.wait(sy)
    return nc



def _stage_inputs(x, w, cfg):
    T, TC, TP, NPRE = cfg.T, cfg.TC, cfg.TP, cfg.NPRE
    xT = np.ascontiguousarray(x.T)
    shared = dict(
        wq=np.ascontiguousarray(w["Wq"]), wk=np.ascontiguousarray(w["Wk"]),
        wv=np.ascontiguousarray(w["Wv"]), wg=np.ascontiguousarray(w["Wg"]),
        ws=np.ascontiguousarray(w["Ws"]), wo=np.ascontiguousarray(w["Wo"]),
        wfcT=np.ascontiguousarray(w["w_fc"].T),
        wprojT=np.ascontiguousarray(w["w_proj"].T),
        gnw=np.ascontiguousarray(w["g_norm_weight"].reshape(1, DV)),
    )
    maps = []
    for c in range(NCORE):
        t0 = c * TC
        xpre_tok = np.zeros((TP, D), np.float32)
        xpre_tok[:t0] = x[:t0]
        xpreT = np.zeros((D, TP), np.float32)
        xpreT[:, :t0] = xT[:, :t0]
        vb = np.full((128, NPRE), -100.0, np.float32)
        for ch in range(NPRE):
            n = min(max(t0 - ch * C, 0), C)
            if n > 0:
                vb[:n, ch] = 0.0
        maps.append(dict(
            xpre_tok=xpre_tok, xpreT=xpreT,
            xown_tok=np.ascontiguousarray(x[t0:t0 + TC]),
            xownT=np.ascontiguousarray(xT[:, t0:t0 + TC]),
            vbias=vb, **shared))
    return maps


def kernel(L0_attn_hidden_states, L0_attn_Wq, L0_attn_Wk, L0_attn_Wv,
           L0_attn_Wg, L0_attn_Ws, L0_attn_Wo, L0_attn_g_norm_weight,
           L0_ffn_w_fc, L0_ffn_w_proj):
    x = np.asarray(L0_attn_hidden_states, np.float32)
    B, T, _ = x.shape
    cfg = Cfg(T)
    w = dict(Wq=L0_attn_Wq, Wk=L0_attn_Wk, Wv=L0_attn_Wv, Wg=L0_attn_Wg,
             Ws=L0_attn_Ws, Wo=L0_attn_Wo,
             g_norm_weight=L0_attn_g_norm_weight,
             w_fc=L0_ffn_w_fc, w_proj=L0_ffn_w_proj)
    w = {k: np.asarray(v, np.float32) for k, v in w.items()}
    maps = _stage_inputs(x[0], w, cfg)
    nc = build_program(cfg, debug=False)
    res = run_bass_kernel_spmd(nc, maps, core_ids=list(range(NCORE)))
    outT = np.concatenate([np.asarray(r["outT"]) for r in res.results], axis=1)
    return np.ascontiguousarray(outT.T).reshape(1, T, D).astype(np.float32)
